# version 12
# baseline (speedup 1.0000x reference)
import numpy as np
import ml_dtypes
from contextlib import ExitStack
from collections import defaultdict

import concourse.bass as bass
import concourse.mybir as mybir
from concourse.bass_utils import run_bass_kernel_spmd

F32 = mybir.dt.float32
BF16 = mybir.dt.bfloat16
U8 = mybir.dt.uint8
AF = mybir.ActivationFunctionType
ALU = mybir.AluOpType
AX = mybir.AxisListType

T = 4096
C = 1024
NTILE = T // 128
NCT = C // 128
CHUNK = 512
NCHUNK = T // CHUNK
TPC = CHUNK // 128
MAGIC = 12582912.0

ENGINES = ("sync", "gpsimd", "tensor", "scalar", "vector")


class SemCtr:
    def __init__(self, handle):
        self.h = handle
        self.n = 0

    def inc(self, amt=1):
        self.n += amt
        return (self.h, amt)

    def at(self):
        return (self.h, self.n)


class LB:
    def __init__(self):
        self.nc = bass.Bass("TRN2", target_bir_lowering=False)
        self.es = ExitStack()
        self.scopes = []
        self.seq = defaultdict(list)
        self.sem_i = 0
        self.ten_i = 0
        self.pfx = ""

    def sem(self, name=None):
        h = self.es.enter_context(self.nc.semaphore(self.pfx + (name or f"s{self.sem_i}")))
        self.sem_i += 1
        return SemCtr(h)

    def _stack(self):
        return self.scopes[-1] if self.scopes else self.es

    def sb(self, name, shape, dtype=F32):
        return self._stack().enter_context(self.nc.sbuf_tensor(self.pfx + name, shape, dtype))

    def ps(self, name, shape, dtype=F32):
        return self._stack().enter_context(self.nc.psum_tensor(self.pfx + name, shape, dtype))

    def push(self):
        self.scopes.append(ExitStack())

    def pop(self):
        self.scopes.pop().close()

    def dram(self, name, shape, dtype=F32, kind="Internal"):
        return self.nc.dram_tensor(name, shape, dtype, kind=kind)

    def init_consts(self, values):
        cp = self.sb("constpool", [128, len(values)])
        self.const_sem = self.sem("constsem")
        for j, v in enumerate(values):
            self.op("vector", lambda e, j=j, v=v: e.memset(cp[:, j:j + 1], float(v)),
                    incs=(self.const_sem.inc(),) if j == len(values) - 1 else ())
            self.nc.const_aps.register(F32, float(v), cp[:, j:j + 1])
        for e in ("sync", "gpsimd", "tensor", "scalar"):
            self.op(e, None, waits=(self.const_sem.at(),))

    def op(self, engine, fn, waits=(), incs=()):
        self.seq[engine].append((tuple(waits), fn, tuple(incs)))

    def barrier(self, engines=ENGINES):
        if not hasattr(self, "_bar"):
            pfx = self.pfx
            self.pfx = ""
            self._bar = self.sem("barrier")
            self.pfx = pfx
        bar = self._bar
        for e in engines:
            self.op(e, lambda eng, b=bar: eng.sem_inc(b.h, 1))
        bar.n += len(engines)
        tgt = bar.at()
        for e in engines:
            self.op(e, None, waits=(tgt,))

    def build(self):
        assert not self.scopes
        with self.es:
            with self.nc.Block() as block:
                def mk(eng_name):
                    items = self.seq[eng_name]
                    def run(eng):
                        for waits, fn, incs in items:
                            for (h, v) in waits:
                                eng.wait_ge(h, v)
                            if fn is None:
                                continue
                            inst = fn(eng)
                            for (h, amt) in incs:
                                assert inst is not None
                                inst = inst.then_inc(h, amt)
                    return run
                block.sync(mk("sync"))
                block.gpsimd(mk("gpsimd"))
                block.tensor(mk("tensor"))
                block.scalar(mk("scalar"))
                block.vector(mk("vector"))
        return self.nc



def emit_vec_cols(lb, name, dram_ap, sem, n=C):
    t = lb.sb(name, [128, n // 128])
    lb.op("sync", lambda e, d=dram_ap, t=t: e.dma_start(
        out=t[:], in_=d.rearrange("(a p) -> p a", p=128), allow_slow_non_contiguous=True),
        incs=(sem.inc(16),))
    lb.op("sync", None, waits=(sem.at(),))
    return t


def emit_bcast_row(lb, name, dram_ap, sem, n=C):
    t = lb.sb(name, [128, n])
    lb.op("sync", lambda e, d=dram_ap, t=t: e.dma_start(
        out=t[:], in_=d.rearrange("(one f) -> one f", one=1).partition_broadcast(128)),
        incs=(sem.inc(16),))
    lb.op("sync", None, waits=(sem.at(),))
    return t


class LNTmp:
    def __init__(self, lb):
        self.s = lb.sb("ln_s", [128, 1])
        self.q = lb.sb("ln_q", [128, 1])
        self.v = lb.sb("ln_v", [128, 1])
        self.m = lb.sb("ln_m", [128, 1])
        self.r = lb.sb("ln_r", [128, 1])


def emit_ln(lb, tmp, xin, h_out, dmp, ac, dv, wait_in=(), d=C, eps=1e-5):
    lb.op("vector", lambda e: e.tensor_scalar(dmp[:], xin[:], 1.0, 0.0, ALU.mult, ALU.add,
                                              accum_out=tmp.s[:]), waits=wait_in, incs=(dv.inc(),))
    lb.op("vector", lambda e: e.scalar_tensor_tensor(dmp[:], xin[:], 1.0, xin[:], ALU.mult, ALU.mult,
                                                     accum_out=tmp.q[:]), incs=(dv.inc(),))
    lb.op("scalar", lambda e: e.activation(tmp.m[:], tmp.s[:], AF.Identity, scale=1.0 / d),
          waits=(dv.at(),))
    lb.op("scalar", lambda e: e.activation(tmp.v[:], tmp.m[:], AF.Square), incs=(ac.inc(),))
    lb.op("vector", lambda e: e.tensor_scalar(tmp.s[:], tmp.q[:], 1.0 / d, tmp.v[:],
                                              ALU.mult, ALU.subtract), waits=(ac.at(),), incs=(dv.inc(),))
    lb.op("scalar", lambda e: e.activation(tmp.v[:], tmp.s[:], AF.Ln, bias=eps), waits=(dv.at(),))
    lb.op("scalar", lambda e: e.activation(tmp.r[:], tmp.v[:], AF.Exp, scale=-0.5), incs=(ac.inc(),))
    lb.op("vector", lambda e: e.tensor_scalar(h_out[:], xin[:], tmp.m[:], tmp.r[:], ALU.subtract, ALU.mult),
          waits=(ac.at(),), incs=(dv.inc(),))


def emit_transpose8(lb, src_f32, dst_cb, ps_pair, idf, pe, ev, dv_wait, dst_dtype_scalar_copy=True):
    for half in range(2):
        pbuf = ps_pair[half]
        for q in range(4):
            ct = half * 4 + q
            w = ()
            if q == 0:
                w = dv_wait + (ev.at(),)
            lb.op("tensor", lambda e, ct=ct, q=q, pbuf=pbuf: e.transpose(
                pbuf[:, q * 128:(q + 1) * 128], src_f32[:, ct * 128:(ct + 1) * 128], idf[:]),
                waits=w, incs=(pe.inc(),) if q == 3 else ())
        lb.op("scalar", lambda e, half=half, pbuf=pbuf: e.copy(dst_cb(half), pbuf[:]),
              waits=(pe.at(),), incs=(ev.inc(),))


def make_common(lb):
    cm = {}
    cm["tmp"] = LNTmp(lb)
    cm["idf"] = lb.sb("cm_idf", [128, 128], F32)
    cm["sem_c"] = lb.sem("cm_c")
    lb.op("sync", lambda e, t=cm["idf"]: e.dma_start(out=t[:], in_=lb.ident_ap), incs=(cm["sem_c"].inc(16),))
    lb.op("sync", None, waits=(cm["sem_c"].at(),))
    cm["xin"] = [lb.sb(f"cm_xin{i}", [128, C]) for i in range(2)]
    cm["dmp"] = lb.sb("cm_dmp", [128, C])
    cm["ha"] = lb.sb("cm_ha", [128, C])
    cm["hb"] = lb.sb("cm_hb", [128, C])
    cm["otok"] = lb.sb("cm_otok", [128, C])
    return cm



def emit_l0_attn(lb, cm, ext, debug=False):
    lb.pfx = "A_"
    x0 = ext["x0"]; wkT = ext["wkT"]; wvT = ext["wvT"]; wrT = ext["wrT"]
    woT = ext["woT"]; wshT = ext["wshT"]; fwkT = ext["fwkT"]; fwvT = ext["fwvT"]
    fwrT = ext["fwrT"]; vecs = ext["vecs"]; x1 = ext["x1"]
    xi_scr = ext["xi_scr"]; kk_scr = ext["kk_scr"]; srr_scr = ext["srr_scr"]
    tmp = cm["tmp"]; idf = cm["idf"]
    xin = cm["xin"]; dmp = cm["dmp"]; h0t = cm["ha"]; h1t = cm["hb"]
    xi_tok = cm["otok"]
    if debug:
        dbg_h = ext["dbg_h"]; dbg_k = ext["dbg_k"]; dbg_ek = ext["dbg_ek"]; dbg_rw = ext["dbg_rw"]
    lb.push()
    aux = [lb.sb(f"aux{i}", [128, C]) for i in range(2)]
    sem_c = cm["sem_c"]
    mk_c = emit_vec_cols(lb, "mk_c", vecs["mixk"], sem_c)
    mv_c = emit_vec_cols(lb, "mv_c", vecs["mixv"], sem_c)
    mr_c = emit_vec_cols(lb, "mr_c", vecs["mixr"], sem_c)
    fk_c = emit_vec_cols(lb, "fk_c", vecs["fmixk"], sem_c)
    fr_c = emit_vec_cols(lb, "fr_c", vecs["fmixr"], sem_c)
    td_c = emit_vec_cols(lb, "td_c", vecs["tdec"], sem_c)
    tf_c = emit_vec_cols(lb, "tf_c", vecs["tfir"], sem_c)
    omk_c = lb.sb("omk_c", [128, NCT]); omv_c = lb.sb("omv_c", [128, NCT])
    omr_c = lb.sb("omr_c", [128, NCT]); ofk_c = lb.sb("ofk_c", [128, NCT])
    ofr_c = lb.sb("ofr_c", [128, NCT])
    eu_c = lb.sb("eu_c", [128, NCT]); ew_c = lb.sb("ew_c", [128, NCT])
    sem_d = lb.sem("deriv")
    for src, dst in ((mk_c, omk_c), (mv_c, omv_c), (mr_c, omr_c), (fk_c, ofk_c), (fr_c, ofr_c)):
        lb.op("vector", lambda e, s=src, d=dst: e.tensor_scalar(d[:], s[:], -1.0, 1.0, ALU.mult, ALU.add),
              waits=(sem_c.at(),), incs=(sem_d.inc(),))
    lb.op("scalar", lambda e: e.activation(eu_c[:], tf_c[:], AF.Exp), waits=(sem_c.at(),), incs=(sem_d.inc(),))
    lb.op("scalar", lambda e: e.activation(ew_c[:], td_c[:], AF.Exp), waits=(sem_c.at(),))
    lb.op("scalar", lambda e: e.activation(ew_c[:], ew_c[:], AF.Exp, scale=-1.0), incs=(sem_d.inc(),))
    zro = lb.sb("zro", [128, CHUNK])
    lb.op("vector", lambda e: e.memset(zro[:], 0.0), incs=(sem_d.inc(),))

    ld = lb.sem("ld"); ac = lb.sem("ac"); dv = lb.sem("dv")
    pe = lb.sem("pe"); ev = lb.sem("ev"); st = lb.sem("st")
    ldS = [lb.sem("ldS0"), lb.sem("ldS1")]
    ldA = [lb.sem("ldA0"), lb.sem("ldA1")]
    lb.op("sync", None, waits=(sem_c.at(),))

    lb.push()
    w1bc = emit_bcast_row(lb, "w1bc", vecs["ln1w"], sem_c)
    b1bc = emit_bcast_row(lb, "b1bc", vecs["ln1b"], sem_c)
    wk_s = lb.sb("wk_s", [128, NCT, C], BF16)
    wv_s = lb.sb("wv_s", [128, NCT, C], BF16)
    wr_s = lb.sb("wr_s", [128, NCT, C], BF16)
    wo_s = lb.sb("wo_s", [128, NCT, C], BF16)
    wsh_s = lb.sb("wsh_s", [128, NCT, C], BF16)
    sem_w = lb.sem("wts")
    for dram_ap, t in ((wkT, wk_s), (wvT, wv_s), (wrT, wr_s), (woT, wo_s), (wshT, wsh_s)):
        lb.op("gpsimd", lambda e, d=dram_ap, t=t: e.dma_start(
            out=t[:], in_=d.rearrange("(a p) o -> p a o", p=128)), incs=(sem_w.inc(16),))

    hTc = lb.sb("hTc", [128, NCT, CHUNK + 1], BF16)
    h0T = lb.sb("h0T", [128, NCT, CHUNK], BF16)
    xkT = lb.sb("xkT", [128, NCT, CHUNK], BF16)
    xvT = lb.sb("xvT", [128, NCT, CHUNK], BF16)
    xrT = lb.sb("xrT", [128, NCT, CHUNK], BF16)
    rwkvT = lb.sb("rwkvT", [128, NCT, CHUNK], BF16)
    xiT = lb.sb("xiT", [128, NCT, CHUNK])
    mixtmp = lb.sb("mixtmp", [128, CHUNK])
    ek_t = [lb.sb(f"ek_t{i}", [128, CHUNK]) for i in range(2)]
    vv_t = [lb.sb(f"vv_t{i}", [128, CHUNK]) for i in range(2)]
    sr_t = [lb.sb(f"sr_t{i}", [128, CHUNK]) for i in range(2)]
    ewb1 = [lb.sb(f"ewb1{i}", [128, CHUNK]) for i in range(2)]
    ekv_t = lb.sb("ekv_t", [128, CHUNK])
    num_t = dmp[:, 0:CHUNK]
    den_t = dmp[:, CHUNK:2 * CHUNK]
    _xif = xiT[:, 5:8, :].rearrange("p a t -> p (a t)")
    Sbuf = _xif[:, 0:CHUNK + 1]
    Dbuf = _xif[:, 768:768 + CHUNK + 1]
    Scar = lb.sb("Scar", [128, NCT]); Dcar = lb.sb("Dcar", [128, NCT])
    ps_b = [lb.ps(f"ps_b{i}", [128, 512]) for i in range(2)]
    ps_k = lb.ps("ps_k", [128, 512]); ps_v = lb.ps("ps_v", [128, 512])
    ps_r = lb.ps("ps_r", [128, 512]); ps_o = lb.ps("ps_o", [128, 512])
    ps_t = [lb.ps(f"ps_t{i}", [128, 512]) for i in range(2)]

    lb.op("vector", lambda e: e.memset(Scar[:], 0.0))
    lb.op("vector", lambda e: e.memset(Dcar[:], 0.0))
    lb.op("vector", lambda e: e.memset(hTc[:, :, 0:1], 0.0), incs=(sem_d.inc(),))

    mx = lb.sem("mx"); kvr = lb.sem("kvr"); evc = lb.sem("evc"); wkvd = lb.sem("wkvd")
    xio = lb.sem("xio"); xie = lb.sem("xie"); dbgs = lb.sem("dbgs"); psf = lb.sem("psf")
    lb._wk_hist = []

    for c in range(NCHUNK):
        if c > 0:
            lb.barrier()
            lb.op("vector", lambda e: e.tensor_copy(hTc[:, :, 0:1], hTc[:, :, CHUNK:CHUNK + 1]),
                  waits=(dbgs.at(),), incs=(dv.inc(),))
        for tj in range(TPC):
            ti = c * TPC + tj
            bi = tj % 2
            wfree = (dv.at(), ac.at()) if tj >= 2 else ()
            lb.op("sync", lambda e, ti=ti, bi=bi: e.dma_start(out=xin[bi][:], in_=x0[ti * 128:(ti + 1) * 128, :]),
                  waits=wfree, incs=(ldS[bi].inc(16),))
            emit_ln(lb, tmp, xin[bi], h0t, dmp, ac, dv,
                    wait_in=(ldS[bi].at(),) + ((pe.at(),) if (c > 0 or tj > 0) else ()))
            emit_transpose8(lb, h0t, lambda half, tj=tj: h0T[:, half * 4:(half + 1) * 4, tj * 128:(tj + 1) * 128],
                            ps_b, idf, pe, ev, (dv.at(), sem_c.at()))
            lb.op("scalar", lambda e: e.activation(tmp.q[:], tmp.r[:], AF.Square), incs=(ac.inc(),))
            lb.op("vector", lambda e: e.tensor_mul(tmp.v[:], tmp.s[:], tmp.q[:]), waits=(ac.at(),),
                  incs=(dv.inc(),))
            lb.op("scalar", lambda e: e.activation(tmp.v[:], tmp.v[:], AF.Ln, bias=1e-5), waits=(dv.at(),))
            lb.op("scalar", lambda e: e.activation(tmp.r[:], tmp.v[:], AF.Exp, scale=-0.5), incs=(ac.inc(),))
            lb.op("vector", lambda e: e.scalar_tensor_tensor(h1t[:], h0t[:], tmp.r[:], w1bc[:],
                                                             ALU.mult, ALU.mult), waits=(ac.at(),))
            lb.op("vector", lambda e: e.tensor_add(h1t[:], h1t[:], b1bc[:]), incs=(dv.inc(),))
            emit_transpose8(lb, h1t, lambda half, tj=tj: hTc[:, half * 4:(half + 1) * 4, 1 + tj * 128:1 + (tj + 1) * 128],
                            ps_b, idf, pe, ev, (dv.at(),))
        for ct in range(NCT):
            h_cur = hTc[:, ct, 1:1 + CHUNK]
            h_shf = hTc[:, ct, 0:CHUNK]
            for (mc, omc, dst) in ((mk_c, omk_c, xkT), (mv_c, omv_c, xvT), (mr_c, omr_c, xrT)):
                lb.op("vector", lambda e, h=h_cur, mc=mc, ct=ct: e.tensor_scalar_mul(
                    mixtmp[:], h, mc[:, ct:ct + 1]),
                    waits=(ev.at(), sem_d.at()) if ct == 0 else ())
                lb.op("vector", lambda e, hs=h_shf, omc=omc, dst=dst, ct=ct: e.scalar_tensor_tensor(
                    dst[:, ct, :], hs, omc[:, ct:ct + 1], mixtmp[:], ALU.mult, ALU.add),
                    incs=(mx.inc(),) if (ct == NCT - 1 and dst is xrT) else ())
        for o in range(NCT):
            for psd, ws, src in ((ps_k, wk_s, xkT), (ps_v, wv_s, xvT), (ps_r, wr_s, xrT)):
                for ct in range(NCT):
                    w = ()
                    if ct == 0 and psd is ps_k:
                        w = (mx.at(), sem_w.at(), psf.at())
                    lb.op("tensor", lambda e, o=o, ct=ct, psd=psd, ws=ws, src=src: e.matmul(
                        psd[:], ws[:, ct, o * 128:(o + 1) * 128], src[:, ct, :],
                        start=(ct == 0), stop=(ct == NCT - 1)),
                        waits=w, incs=(kvr.inc(),) if (ct == NCT - 1 and psd is ps_r) else ())
            po = o % 2
            wev = ((wkvd.h, lb._wk_hist[-2]),) if len(lb._wk_hist) >= 2 else ()
            lb.op("scalar", lambda e, po=po: e.activation(ek_t[po][:], ps_k[:], AF.Exp),
                  waits=(kvr.at(),) + wev)
            lb.op("scalar", lambda e, po=po: e.copy(vv_t[po][:], ps_v[:]))
            lb.op("scalar", lambda e, po=po: e.activation(sr_t[po][:], ps_r[:], AF.Exp, scale=-1.0),
                  incs=(psf.inc(),))
            lb.op("scalar", lambda e, po=po: e.activation(sr_t[po][:], sr_t[po][:], AF.Ln, bias=1.0))
            lb.op("scalar", lambda e, po=po: e.activation(sr_t[po][:], sr_t[po][:], AF.Exp, scale=-1.0))
            lb.op("scalar", lambda e, o=o, po=po: e.activation(ewb1[po][:], zro[:], AF.Identity,
                                                               bias=ew_c[:, o:o + 1]),
                  waits=(sem_d.at(),), incs=(evc.inc(),))
            lb.op("vector", lambda e, o=o: e.tensor_copy(Sbuf[:, 0:1], Scar[:, o:o + 1]), waits=(evc.at(),))
            lb.op("vector", lambda e, o=o: e.tensor_copy(Dbuf[:, 0:1], Dcar[:, o:o + 1]))
            lb.op("vector", lambda e, po=po: e.tensor_mul(ekv_t[:], ek_t[po][:], vv_t[po][:]))
            lb.op("vector", lambda e, o=o, po=po: e.tensor_tensor_scan(
                Sbuf[:, 1:], ewb1[po][:], ekv_t[:], Scar[:, o:o + 1], ALU.mult, ALU.add))
            lb.op("vector", lambda e, o=o, po=po: e.tensor_tensor_scan(
                Dbuf[:, 1:], ewb1[po][:], ek_t[po][:], Dcar[:, o:o + 1], ALU.mult, ALU.add))
            lb.op("vector", lambda e, o=o: e.tensor_copy(Scar[:, o:o + 1], Sbuf[:, CHUNK:CHUNK + 1]))
            lb.op("vector", lambda e, o=o: e.scalar_tensor_tensor(
                num_t[:], ekv_t[:], eu_c[:, o:o + 1], Sbuf[:, 0:CHUNK], ALU.mult, ALU.add))
            lb.op("vector", lambda e, o=o: e.tensor_copy(Dcar[:, o:o + 1], Dbuf[:, CHUNK:CHUNK + 1]))
            lb.op("vector", lambda e, o=o, po=po: e.scalar_tensor_tensor(
                den_t[:], ek_t[po][:], eu_c[:, o:o + 1], Dbuf[:, 0:CHUNK], ALU.mult, ALU.add))
            lb.op("vector", lambda e: e.reciprocal(den_t[:], den_t[:]))
            lb.op("vector", lambda e: e.tensor_mul(num_t[:], num_t[:], den_t[:]))
            lb.op("vector", lambda e, o=o, po=po: e.tensor_mul(rwkvT[:, o, :], num_t[:], sr_t[po][:]),
                  incs=(wkvd.inc(),))
            lb._wk_hist.append(wkvd.n)
        for o in range(NCT):
            for ct in range(NCT):
                w = (wkvd.at(), xie.at()) if ct == 0 else ()
                lb.op("tensor", lambda e, o=o, ct=ct: e.matmul(
                    ps_o[:], wo_s[:, ct, o * 128:(o + 1) * 128], rwkvT[:, ct, :],
                    start=(ct == 0), stop=False), waits=w)
            for ct in range(NCT):
                lb.op("tensor", lambda e, o=o, ct=ct: e.matmul(
                    ps_o[:], wsh_s[:, ct, o * 128:(o + 1) * 128], h0T[:, ct, :],
                    start=False, stop=(ct == NCT - 1)),
                    incs=(xio.inc(),) if ct == NCT - 1 else ())
            lb.op("vector", lambda e, o=o: e.tensor_copy(xiT[:, o, :], ps_o[:]),
                  waits=(xio.at(),), incs=(xie.inc(),))
        for tj in range(TPC):
            ti = c * TPC + tj
            for half in range(2):
                pbuf = ps_t[half]
                for q in range(4):
                    ct = half * 4 + q
                    w = (xie.at(), st.at(), ev.at()) if q == 0 else ()
                    lb.op("tensor", lambda e, tj=tj, ct=ct, q=q, pbuf=pbuf: e.transpose(
                        pbuf[:, q * 128:(q + 1) * 128], xiT[:, ct, tj * 128:(tj + 1) * 128], idf[:]),
                        waits=w, incs=(pe.inc(),) if q == 3 else ())
                lb.op("scalar", lambda e, half=half, pbuf=pbuf: e.copy(
                    xi_tok[:, half * 512:(half + 1) * 512], pbuf[:]),
                    waits=(pe.at(),), incs=(ev.inc(),))
            lb.op("sync", lambda e, ti=ti: e.dma_start(
                out=xi_scr[ti * 128:(ti + 1) * 128, :], in_=xi_tok[:]),
                waits=(ev.at(),), incs=(st.inc(16),))
        if debug and c == 0:
            lb.op("sync", lambda e: e.dma_start(out=dbg_h, in_=hTc[:]), waits=(ev.at(),), incs=(dbgs.inc(16),))
            lb.op("sync", lambda e: e.dma_start(out=dbg_k, in_=xkT[:]), waits=(mx.at(),), incs=(dbgs.inc(16),))
            lb.op("sync", lambda e: e.dma_start(out=dbg_rw, in_=rwkvT[:]), waits=(wkvd.at(),), incs=(dbgs.inc(16),))
            lb.op("sync", lambda e: e.dma_start(out=dbg_ek[:, 0, :], in_=Sbuf[:, 1:]), waits=(wkvd.at(),), incs=(dbgs.inc(16),))
            lb.op("sync", lambda e: e.dma_start(out=dbg_ek[:, 1, :], in_=Dbuf[:, 1:]), incs=(dbgs.inc(16),))
            lb.op("sync", lambda e: e.dma_start(out=dbg_ek[:, 2, :], in_=ewb1[:]), incs=(dbgs.inc(16),))
    lb.op("sync", lambda e: e.nop(), waits=(st.at(),))
    lb.barrier()
    lb.pop()

    lb.push()
    w2bc = emit_bcast_row(lb, "w2bc", vecs["ln2w"], sem_c)
    b2bc = emit_bcast_row(lb, "b2bc", vecs["ln2b"], sem_c)
    fwk_s = lb.sb("fwk_s", [128, NCT, 4 * C], BF16)
    fwr_s = lb.sb("fwr_s", [128, NCT, C], BF16)
    sem_fw = lb.sem("fw")
    lb.op("gpsimd", lambda e: e.dma_start(out=fwk_s[:], in_=fwkT.rearrange("(a p) o -> p a o", p=128)),
          incs=(sem_fw.inc(16),))
    lb.op("gpsimd", lambda e: e.dma_start(out=fwr_s[:], in_=fwrT.rearrange("(a p) o -> p a o", p=128)),
          incs=(sem_fw.inc(16),))
    gTc = lb.sb("gTc", [128, NCT, CHUNK + 1], BF16)
    gkT = lb.sb("gkT", [128, NCT, CHUNK], BF16)
    grT = lb.sb("grT", [128, NCT, CHUNK], BF16)
    mixtmp2 = lb.sb("mixtmp2", [128, CHUNK])
    rl_p = [lb.sb(f"rl_p{i}", [128, CHUNK]) for i in range(2)]
    sg_t = lb.sb("sg_t", [128, CHUNK])
    kkb = [lb.sb(f"kkb{i}", [128, CHUNK], BF16) for i in range(4)]
    ps2_b = [lb.ps(f"ps2_b{i}", [128, 512]) for i in range(2)]
    ps2_k = [lb.ps(f"ps2_k{i}", [128, 512]) for i in range(2)]
    ps2_r = lb.ps("ps2_r", [128, 512])
    kkS = [lb.sem(f"kkS{i}") for i in range(4)]
    mx2 = lb.sem("mx2"); kf = lb.sem("kf"); rlv = lb.sem("rlv")
    lb._rl_hist = []

    lb.op("vector", lambda e: e.memset(gTc[:, :, 0:1], 0.0), incs=(dv.inc(),))
    for c in range(NCHUNK):
        if c > 0:
            lb.barrier()
            lb.op("vector", lambda e: e.tensor_copy(gTc[:, :, 0:1], gTc[:, :, CHUNK:CHUNK + 1]),
                  incs=(dv.inc(),))
        for tj in range(TPC):
            ti = c * TPC + tj
            bi = tj % 2
            wfree = (dv.at(), ac.at()) if tj >= 2 else ()
            lb.op("sync", lambda e, ti=ti, bi=bi: e.dma_start(out=xin[bi][:], in_=xi_scr[ti * 128:(ti + 1) * 128, :]),
                  waits=wfree, incs=(ldS[bi].inc(16),))
            emit_ln(lb, tmp, xin[bi], h1t, dmp, ac, dv,
                    wait_in=(ldS[bi].at(),) + ((pe.at(),) if (c > 0 or tj > 0) else ()))
            lb.op("vector", lambda e: e.tensor_mul(h1t[:], h1t[:], w2bc[:]))
            lb.op("vector", lambda e: e.tensor_add(h1t[:], h1t[:], b2bc[:]), incs=(dv.inc(),))
            emit_transpose8(lb, h1t, lambda half, tj=tj: gTc[:, half * 4:(half + 1) * 4, 1 + tj * 128:1 + (tj + 1) * 128],
                            ps2_b, idf, pe, ev, (dv.at(),))
        for ct in range(NCT):
            h_cur = gTc[:, ct, 1:1 + CHUNK]
            h_shf = gTc[:, ct, 0:CHUNK]
            for (mc, omc, dst) in ((fk_c, ofk_c, gkT), (fr_c, ofr_c, grT)):
                lb.op("vector", lambda e, h=h_cur, mc=mc, ct=ct: e.tensor_scalar_mul(
                    mixtmp2[:], h, mc[:, ct:ct + 1]), waits=(ev.at(),) if ct == 0 else ())
                lb.op("vector", lambda e, hs=h_shf, omc=omc, dst=dst, ct=ct: e.scalar_tensor_tensor(
                    dst[:, ct, :], hs, omc[:, ct:ct + 1], mixtmp2[:], ALU.mult, ALU.add),
                    incs=(mx2.inc(),) if (ct == NCT - 1 and dst is grT) else ())
        for f in range(4 * C // 128):
            bf = f % 2
            psd = ps2_k[bf]
            for ct in range(NCT):
                w = (mx2.at(), sem_fw.at(), evc.at()) if ct == 0 else ()
                lb.op("tensor", lambda e, f=f, ct=ct, psd=psd: e.matmul(
                    psd[:], fwk_s[:, ct, f * 128:(f + 1) * 128], gkT[:, ct, :],
                    start=(ct == 0), stop=(ct == NCT - 1)),
                    waits=w, incs=(kf.inc(),) if ct == NCT - 1 else ())
            wrl2 = ((rlv.h, lb._rl_hist[-2]),) if len(lb._rl_hist) >= 2 else ()
            lb.op("scalar", lambda e, psd=psd, bf=bf: e.activation(rl_p[bf][:], psd[:], AF.Relu),
                  waits=(kf.at(),) + wrl2, incs=(evc.inc(),))
            sl = f % 4
            lb.op("vector", lambda e, bf=bf, sl=sl: e.tensor_mul(kkb[sl][:], rl_p[bf][:], rl_p[bf][:]),
                  waits=(evc.at(), kkS[sl].at()), incs=(rlv.inc(),))
            lb._rl_hist.append(rlv.n)
            lb.op("sync", lambda e, c=c, f=f, sl=sl: e.dma_start(
                out=kk_scr[f * 128:(f + 1) * 128, c * CHUNK:(c + 1) * CHUNK], in_=kkb[sl][:]),
                waits=(rlv.at(),), incs=(kkS[sl].inc(16),))
        for o in range(NCT):
            for ct in range(NCT):
                w = (mx2.at(), evc.at()) if ct == 0 else ()
                lb.op("tensor", lambda e, o=o, ct=ct: e.matmul(
                    ps2_r[:], fwr_s[:, ct, o * 128:(o + 1) * 128], grT[:, ct, :],
                    start=(ct == 0), stop=(ct == NCT - 1)),
                    waits=w, incs=(kf.inc(),) if ct == NCT - 1 else ())
            lb.op("scalar", lambda e, o=o: e.activation(sg_t[:], ps2_r[:], AF.Exp, scale=-1.0),
                  waits=(kf.at(), rlv.at()))
            lb.op("scalar", lambda e, o=o: e.activation(sg_t[:], sg_t[:], AF.Ln, bias=1.0))
            lb.op("scalar", lambda e, o=o: e.activation(sg_t[:], sg_t[:], AF.Exp, scale=-1.0),
                  incs=(evc.inc(),))
            sl = o % 4
            lb.op("vector", lambda e, sl=sl: e.tensor_copy(kkb[sl][:], sg_t[:]),
                  waits=(evc.at(), kkS[sl].at()), incs=(rlv.inc(),))
            lb.op("sync", lambda e, c=c, o=o, sl=sl: e.dma_start(
                out=srr_scr[o * 128:(o + 1) * 128, c * CHUNK:(c + 1) * CHUNK], in_=kkb[sl][:]),
                waits=(rlv.at(),), incs=(kkS[sl].inc(16),))
    lb.op("sync", lambda e: e.nop(), waits=tuple(s.at() for s in kkS))
    lb.barrier()
    lb.pop()

    lb.push()
    fwv_s = lb.sb("fwv_s", [128, 32, C], BF16)
    sem_fv = lb.sem("fv")
    lb.op("gpsimd", lambda e: e.dma_start(out=fwv_s[:], in_=fwvT.rearrange("(a p) o -> p a o", p=128)),
          incs=(sem_fv.inc(16),))
    kkc = lb.sb("kkc", [128, 32, CHUNK], BF16)
    srrc = lb.sb("srrc", [128, NCT, CHUNK], BF16)
    rkvT = lb.sb("rkvT", [128, NCT, CHUNK])
    ps3_o = lb.ps("ps3_o", [128, 512])
    ps3_t = [lb.ps(f"ps3_t{i}", [128, 512]) for i in range(2)]
    ldk = lb.sem("ldk"); kvo = lb.sem("kvo"); kve = lb.sem("kve")

    for c in range(NCHUNK):
        if c > 0:
            lb.barrier()
        lb.op("sync", lambda e, c=c: e.dma_start(
            out=kkc[:], in_=kk_scr.rearrange("(a p) t -> p a t", p=128)[:, :, c * CHUNK:(c + 1) * CHUNK]),
            incs=(ldk.inc(16),))
        lb.op("sync", lambda e, c=c: e.dma_start(
            out=srrc[:], in_=srr_scr.rearrange("(a p) t -> p a t", p=128)[:, :, c * CHUNK:(c + 1) * CHUNK]),
            incs=(ldk.inc(16),))
        for o in range(NCT):
            for f in range(32):
                w = (ldk.at(), sem_fv.at(), kve.at()) if f == 0 else ()
                lb.op("tensor", lambda e, o=o, f=f: e.matmul(
                    ps3_o[:], fwv_s[:, f, o * 128:(o + 1) * 128], kkc[:, f, :],
                    start=(f == 0), stop=(f == 31)),
                    waits=w, incs=(kvo.inc(),) if f == 31 else ())
            lb.op("vector", lambda e, o=o: e.tensor_mul(rkvT[:, o, :], srrc[:, o, :], ps3_o[:]),
                  waits=(kvo.at(),), incs=(kve.inc(),))
        for tj in range(TPC):
            ti = c * TPC + tj
            lb.op("sync", lambda e, ti=ti, tj=tj: e.dma_start(
                out=xin[tj % 2][:], in_=x0[ti * 128:(ti + 1) * 128, :]),
                waits=(dv.at(),) if tj >= 2 else (), incs=(ldS[tj % 2].inc(16),))
            lb.op("sync", lambda e, ti=ti, tj=tj: e.dma_start(
                out=aux[tj % 2][:], in_=xi_scr[ti * 128:(ti + 1) * 128, :]), incs=(ldA[tj % 2].inc(16),))
            for half in range(2):
                pbuf = ps3_t[half]
                for q in range(4):
                    ct = half * 4 + q
                    w = (kve.at(), st.at(), ev.at()) if q == 0 else ()
                    lb.op("tensor", lambda e, tj=tj, ct=ct, q=q, pbuf=pbuf: e.transpose(
                        pbuf[:, q * 128:(q + 1) * 128], rkvT[:, ct, tj * 128:(tj + 1) * 128], idf[:]),
                        waits=w, incs=(pe.inc(),) if q == 3 else ())
                lb.op("scalar", lambda e, half=half, pbuf=pbuf: e.copy(
                    xi_tok[:, half * 512:(half + 1) * 512], pbuf[:]),
                    waits=(pe.at(),), incs=(ev.inc(),))
            lb.op("vector", lambda e, tj=tj: e.tensor_add(xi_tok[:], aux[tj % 2][:], xi_tok[:]),
                  waits=(ev.at(), ldS[tj % 2].at(), ldA[tj % 2].at()))
            lb.op("vector", lambda e, tj=tj: e.tensor_add(xi_tok[:], xin[tj % 2][:], xi_tok[:]),
                  incs=(dv.inc(),))
            lb.op("sync", lambda e, ti=ti: e.dma_start(out=x1[ti * 128:(ti + 1) * 128, :], in_=xi_tok[:]),
                  waits=(dv.at(),), incs=(st.inc(16),))
    lb.op("sync", lambda e: e.nop(), waits=(st.at(),))
    lb.pop()
    lb.pop()


def decl_l0_attn(lb, debug=False, x_ap=None, out_ap=None):
    dbg_kind = "ExternalOutput" if debug else "Internal"
    ext = {}
    ext["x0"] = x_ap if x_ap is not None else lb.dram("x0", [T, C], F32, kind="ExternalInput").ap()
    for n, shp in (("wkT", [C, C]), ("wvT", [C, C]), ("wrT", [C, C]), ("woT", [C, C]),
                   ("wshT", [C, C]), ("fwkT", [C, 4 * C]), ("fwvT", [4 * C, C]), ("fwrT", [C, C])):
        ext[n] = lb.dram(n, shp, F32, kind="ExternalInput").ap()
    ext["vecs"] = {n: lb.dram(n, [C], F32, kind="ExternalInput").ap() for n in
                   ("ln1w", "ln1b", "ln2w", "ln2b", "mixk", "mixv", "mixr", "fmixk", "fmixr", "tdec", "tfir")}
    ext["x1"] = out_ap if out_ap is not None else lb.dram("x1", [T, C], F32, kind="ExternalOutput").ap()
    ext["xi_scr"] = lb.dram("xi_scr", [T, C], F32, kind=dbg_kind).ap()
    ext["kk_scr"] = lb.dram("kk_scr", [4 * C, T], BF16, kind=dbg_kind).ap()
    ext["srr_scr"] = lb.dram("srr_scr", [C, T], BF16, kind=dbg_kind).ap()
    if debug:
        ext["dbg_h"] = lb.dram("dbg_h", [128, NCT, CHUNK + 1], BF16, kind="ExternalOutput").ap()
        ext["dbg_k"] = lb.dram("dbg_k", [128, NCT, CHUNK], BF16, kind="ExternalOutput").ap()
        ext["dbg_ek"] = lb.dram("dbg_ek", [128, 3, CHUNK], F32, kind="ExternalOutput").ap()
        ext["dbg_rw"] = lb.dram("dbg_rw", [128, NCT, CHUNK], BF16, kind="ExternalOutput").ap()
    return ext


def build_l0_attn(debug=False):
    lb = LB()
    lb.ident_ap = lb.dram("i128f", [128, 128], F32, kind="ExternalInput").ap()
    lb.init_consts([1e-5, -0.5, -1.0, 1.0, 1.0 / 1024.0, LN8])
    ext = decl_l0_attn(lb, debug=debug)
    cm = make_common(lb)
    emit_l0_attn(lb, cm, ext, debug=debug)
    return lb.build()


def host_prep_l0attn(inp, x_core):
    ct = np.ascontiguousarray
    return {
        "x0": x_core,
        "wkT": ct(inp["L0_attn_att_Wk"].T), "wvT": ct(inp["L0_attn_att_Wv"].T),
        "wrT": ct(inp["L0_attn_att_Wr"].T), "woT": ct(inp["L0_attn_att_Wo"].T),
        "wshT": ct(inp["L0_attn_short_W"].T),
        "fwkT": ct(inp["L0_attn_ffn_Wk"].T), "fwvT": ct(inp["L0_attn_ffn_Wv"].T),
        "fwrT": ct(inp["L0_attn_ffn_Wr"].T),
        "ln1w": inp["L0_attn_ln1_w"], "ln1b": inp["L0_attn_ln1_b"],
        "ln2w": inp["L0_attn_ln2_w"], "ln2b": inp["L0_attn_ln2_b"],
        "mixk": inp["L0_attn_att_mix_k"], "mixv": inp["L0_attn_att_mix_v"],
        "mixr": inp["L0_attn_att_mix_r"],
        "fmixk": inp["L0_attn_ffn_mix_k"], "fmixr": inp["L0_attn_ffn_mix_r"],
        "tdec": inp["L0_attn_att_time_decay"], "tfir": inp["L0_attn_att_time_first"],
        "i128f": np.eye(128, dtype=np.float32),
    }


LN8 = 2.0794415416798357

def emit_mxq(lb, t_in, out_bf, bufs, ac, dv, wait_in=()):
    ab = bufs["ab"]; wk = bufs["wk"]; sc = bufs["sc"]; rs8 = bufs["rs8"]; s8 = bufs["s8"]
    y8 = bufs["y8"]; r8 = bufs["r8"]; y4 = bufs["y4"]; r4 = bufs["r4"]
    m1 = bufs["m1"]; m2 = bufs["m2"]
    v3 = lambda ap, g: ap.rearrange("p (b g) -> p b g", g=g)
    lb.op("scalar", lambda e: e.activation(ab[:], t_in[:], AF.Abs), waits=wait_in + (dv.at(),),
          incs=(ac.inc(),))
    lb.op("vector", lambda e: e.reduce_max(wk[:, 0:32], v3(ab[:], 32), axis=AX.X), waits=(ac.at(),))
    lb.op("vector", lambda e: e.tensor_scalar_mul(wk[:, 64:576], ab[:, 0:512], 1.0))
    lb.op("vector", lambda e: e.tensor_scalar(sc[:], wk[:, 0:32], 1.0 / 7.5, 1e-30,
                                              ALU.mult, ALU.max))
    lb.op("vector", lambda e: e.tensor_scalar_mul(wk[:, 64:576], ab[:, 0:512], 1.0), incs=(dv.inc(),))
    lb.op("scalar", lambda e: e.activation(rs8[:], sc[:], AF.Ln), waits=(dv.at(),))
    lb.op("scalar", lambda e: e.activation(rs8[:], rs8[:], AF.Exp, scale=-1.0, bias=LN8))
    lb.op("scalar", lambda e: e.activation(s8[:], sc[:], AF.Identity, scale=0.125), incs=(ac.inc(),))
    lb.op("vector", lambda e: e.tensor_tensor(v3(y8[:], 32), v3(t_in[:], 32),
                                              rs8[:].unsqueeze(2).broadcast_to([128, 32, 32]), ALU.mult),
          waits=(ac.at(),))
    lb.op("vector", lambda e: e.tensor_scalar(r8[:], y8[:], MAGIC, MAGIC, ALU.add, ALU.subtract))
    lb.op("vector", lambda e: e.tensor_scalar_mul(y4[:], y8[:], 0.5))
    lb.op("vector", lambda e: e.tensor_scalar(r4[:], y4[:], MAGIC, MAGIC, ALU.add, ALU.subtract))
    lb.op("vector", lambda e: e.tensor_scalar_mul(y4[:], y8[:], 0.25))
    lb.op("vector", lambda e: e.tensor_scalar(ab[:], y4[:], MAGIC, MAGIC, ALU.add, ALU.subtract),
          incs=(dv.inc(),))
    lb.op("scalar", lambda e: e.activation(y4[:], y8[:], AF.Abs), waits=(dv.at(),),
          incs=(ac.inc(),))
    lb.op("vector", lambda e: e.tensor_single_scalar(m1[:], y4[:], 16.0, ALU.is_lt), waits=(ac.at(),))
    lb.op("vector", lambda e: e.tensor_single_scalar(m2[:], y4[:], 32.0, ALU.is_lt))
    lb.op("vector", lambda e: e.tensor_scalar_mul(r4[:], r4[:], 2.0))
    lb.op("vector", lambda e: e.tensor_scalar_mul(ab[:], ab[:], 4.0))
    lb.op("vector", lambda e: e.select(y4[:], m2[:], r4[:], ab[:]))
    lb.op("vector", lambda e: e.select(y8[:], m1[:], r8[:], y4[:]))
    lb.op("vector", lambda e: e.tensor_tensor(v3(out_bf[:], 32), v3(y8[:], 32),
                                              s8[:].unsqueeze(2).broadcast_to([128, 32, 32]), ALU.mult),
          incs=(dv.inc(),))



def emit_l0_ffn(lb, cm, ext):
    lb.pfx = "B_"
    x1 = ext["x1"]; wgt = ext["wgt"]; bias = ext["bias"]; x2 = ext["x2"]
    tmp = cm["tmp"]; idf = cm["idf"]
    xin = cm["xin"]; dmp = cm["dmp"]; h_t = cm["ha"]; qd = cm["hb"]
    out_tok = cm["otok"]
    lb.push()
    sem_c = cm["sem_c"]
    bias_c = emit_vec_cols(lb, "bias_c", bias, sem_c)

    qb = {k: lb.sb("qb_" + k, [128, 1024]) for k in ("ab", "wk", "y8", "r8", "y4", "r4")}
    qb["sc"] = lb.sb("qb_sc", [128, 32]); qb["rs8"] = lb.sb("qb_rs8", [128, 32])
    qb["s8"] = lb.sb("qb_s8", [128, 32])
    qb["m1"] = lb.sb("qb_m1", [128, 1024], U8); qb["m2"] = lb.sb("qb_m2", [128, 1024], U8)
    wdT = lb.sb("wdT", [128, NCT, C], BF16)
    xdT = lb.sb("xdT", [128, NCT, CHUNK], BF16)
    outT = lb.sb("outT", [128, NCT, CHUNK])
    ps_b = [lb.ps(f"ps_b{i}", [128, 512]) for i in range(2)]
    ps_o = lb.ps("ps_o", [128, 512])
    ps_t = [lb.ps(f"ps_t{i}", [128, 512]) for i in range(2)]
    ld = lb.sem("ld"); ac = lb.sem("ac"); dv = lb.sem("dv")
    pe = lb.sem("pe"); ev = lb.sem("ev"); st = lb.sem("st")
    mmo = lb.sem("mmo"); mme = lb.sem("mme"); trd = lb.sem("trd")
    ldS = [lb.sem("ldS0"), lb.sem("ldS1")]

    for j in range(NCT):
        bi = j % 2
        wfree = (dv.at(), ac.at()) if j >= 2 else ()
        lb.op("sync", lambda e, j=j, bi=bi: e.dma_start(out=xin[bi][:], in_=wgt[j * 128:(j + 1) * 128, :]),
              waits=wfree, incs=(ldS[bi].inc(16),))
        emit_mxq(lb, xin[bi], qd, qb, ac, dv, wait_in=(ldS[bi].at(),) + ((pe.at(),) if j > 0 else ()))
        for half in range(2):
            pbuf = ps_b[half]
            for q in range(4):
                ct = half * 4 + q
                w = (dv.at(), sem_c.at(), ev.at()) if q == 0 else ()
                lb.op("tensor", lambda e, ct=ct, q=q, pbuf=pbuf: e.transpose(
                    pbuf[:, q * 128:(q + 1) * 128], qd[:, ct * 128:(ct + 1) * 128], idf[:]),
                    waits=w, incs=(pe.inc(),) if q == 3 else ())
            lb.op("scalar", lambda e, j=j, half=half, pbuf=pbuf: e.copy(
                wdT[:, half * 4:(half + 1) * 4, j * 128:(j + 1) * 128], pbuf[:]),
                waits=(pe.at(),), incs=(ev.inc(),))
    lb.barrier()

    for c in range(NCHUNK):
        if c > 0:
            lb.barrier()
        for tj in range(TPC):
            ti = c * TPC + tj
            bi = tj % 2
            wfree = (dv.at(), ac.at()) if tj >= 2 else ()
            lb.op("sync", lambda e, ti=ti, bi=bi: e.dma_start(out=xin[bi][:], in_=x1[ti * 128:(ti + 1) * 128, :]),
                  waits=wfree, incs=(ldS[bi].inc(16),))
            emit_ln(lb, tmp, xin[bi], h_t, dmp, ac, dv,
                    wait_in=(ldS[bi].at(),) + ((pe.at(),) if (c > 0 or tj > 0) else ()))
            emit_mxq(lb, h_t, qd, qb, ac, dv)
            for half in range(2):
                pbuf = ps_b[half]
                for q in range(4):
                    ct = half * 4 + q
                    w = (dv.at(), ev.at()) if q == 0 else ()
                    lb.op("tensor", lambda e, tj=tj, ct=ct, q=q, pbuf=pbuf: e.transpose(
                        pbuf[:, q * 128:(q + 1) * 128], qd[:, ct * 128:(ct + 1) * 128], idf[:]),
                        waits=w, incs=(pe.inc(),) if q == 3 else ())
                lb.op("scalar", lambda e, tj=tj, half=half, pbuf=pbuf: e.copy(
                    xdT[:, half * 4:(half + 1) * 4, tj * 128:(tj + 1) * 128], pbuf[:]),
                    waits=(pe.at(),), incs=(ev.inc(),))
        for o in range(NCT):
            for ct in range(NCT):
                w = (ev.at(), mme.at()) if ct == 0 else ()
                lb.op("tensor", lambda e, o=o, ct=ct: e.matmul(
                    ps_o[:], wdT[:, ct, o * 128:(o + 1) * 128], xdT[:, ct, :],
                    start=(ct == 0), stop=(ct == NCT - 1)),
                    waits=w, incs=(mmo.inc(),) if ct == NCT - 1 else ())
            lb.op("scalar", lambda e, o=o: e.activation(outT[:, o, :], ps_o[:], AF.Identity,
                                                        bias=bias_c[:, o:o + 1]),
                  waits=(mmo.at(),), incs=(mme.inc(),))
        for tj in range(TPC):
            ti = c * TPC + tj
            bi = tj % 2
            lb.op("sync", lambda e, ti=ti, bi=bi: e.dma_start(out=xin[bi][:], in_=x1[ti * 128:(ti + 1) * 128, :]),
                  waits=(st.at(),) if ti >= 2 else (), incs=(ldS[bi].inc(16),))
            for half in range(2):
                pbuf = ps_t[half]
                for q in range(4):
                    ct = half * 4 + q
                    w = (mme.at(), st.at(), trd.at()) if q == 0 else ()
                    lb.op("tensor", lambda e, tj=tj, ct=ct, q=q, pbuf=pbuf: e.transpose(
                        pbuf[:, q * 128:(q + 1) * 128], outT[:, ct, tj * 128:(tj + 1) * 128], idf[:]),
                        waits=w, incs=(pe.inc(),) if q == 3 else ())
                lb.op("scalar", lambda e, half=half, pbuf=pbuf: e.copy(
                    out_tok[:, half * 512:(half + 1) * 512], pbuf[:]),
                    waits=(pe.at(),), incs=(trd.inc(),))
            lb.op("vector", lambda e, bi=bi: e.tensor_add(out_tok[:], xin[bi][:], out_tok[:]),
                  waits=(trd.at(), ldS[bi].at()), incs=(dv.inc(),))
            lb.op("sync", lambda e, ti=ti: e.dma_start(out=x2[ti * 128:(ti + 1) * 128, :], in_=out_tok[:]),
                  waits=(dv.at(),), incs=(st.inc(16),))
    lb.op("sync", lambda e: e.nop(), waits=(st.at(),))
    lb.pop()


def decl_l0_ffn(lb, x_ap=None, out_ap=None):
    ext = {}
    ext["x1"] = x_ap if x_ap is not None else lb.dram("x1", [T, C], F32, kind="ExternalInput").ap()
    ext["wgt"] = lb.dram("wgt", [C, C], F32, kind="ExternalInput").ap()
    ext["bias"] = lb.dram("bias", [C], F32, kind="ExternalInput").ap()
    ext["x2"] = out_ap if out_ap is not None else lb.dram("x2", [T, C], F32, kind="ExternalOutput").ap()
    return ext


def build_l0_ffn():
    lb = LB()
    lb.ident_ap = lb.dram("i128f", [128, 128], F32, kind="ExternalInput").ap()
    lb.init_consts([1e-5, -0.5, -1.0, 1.0, 1.0 / 1024.0, LN8])
    ext = decl_l0_ffn(lb)
    cm = make_common(lb)
    emit_l0_ffn(lb, cm, ext)
    return lb.build()


def host_prep_l0ffn(inp, x1_core):
    return {
        "x1": x1_core,
        "wgt": inp["L0_ffn_weight"],
        "bias": inp["L0_ffn_bias"],
        "i128f": np.eye(128, dtype=np.float32),
    }


DI = 2048
NDT = DI // 128
NST = 16

def emit_l1_attn(lb, cm, ext):
    lb.pfx = "C_"
    x2 = ext["x2"]; win = ext["win"]; bin_ = ext["bin"]; wx = ext["wx"]; bx = ext["bx"]
    wdt = ext["wdt"]; bdt = ext["bdt"]; alog = ext["alog"]; dp_v = ext["dp_v"]
    wout = ext["wout"]; bout = ext["bout"]; sel16 = ext["sel16"]; x3 = ext["x3"]
    tmp = cm["tmp"]; idf = cm["idf"]
    xin = cm["xin"]; dmp = cm["dmp"]; h_t = cm["ha"]; out_tok = cm["otok"]
    lb.push()
    sem_c = cm["sem_c"]
    binc = emit_vec_cols(lb, "binc", bin_, sem_c, n=2 * DI)
    bdtc = emit_vec_cols(lb, "bdtc", bdt, sem_c, n=DI)
    dpc = emit_vec_cols(lb, "dpc", dp_v, sem_c, n=DI)
    boutc = emit_vec_cols(lb, "boutc", bout, sem_c, n=C)
    bx_s = lb.sb("bx_s", [128, 3])
    lb.op("sync", lambda e: e.dma_start(out=bx_s[0:64, 0:1], in_=bx[0:64].rearrange("(p one) -> p one", one=1)),
          incs=(sem_c.inc(16),))
    lb.op("sync", lambda e: e.dma_start(out=bx_s[0:16, 1:2], in_=bx[64:80].rearrange("(p one) -> p one", one=1)),
          incs=(sem_c.inc(16),))
    lb.op("sync", lambda e: e.dma_start(out=bx_s[0:16, 2:3], in_=bx[80:96].rearrange("(p one) -> p one", one=1)),
          incs=(sem_c.inc(16),))
    sel_s = lb.sb("sel_s", [16, 16 * 128], BF16)
    lb.op("sync", lambda e: e.dma_start(out=sel_s[:], in_=sel16), incs=(sem_c.inc(16),))
    a_s = lb.sb("a_s", [128, NDT, NST])
    lb.op("sync", lambda e: e.dma_start(out=a_s[:], in_=alog.rearrange("(a p) n -> p a n", p=128)),
          incs=(sem_c.inc(16),))
    lb.op("sync", None, waits=(sem_c.at(),))
    sem_a = lb.sem("a")
    lb.op("scalar", lambda e: e.activation(a_s[:], a_s[:], AF.Exp), waits=(sem_c.at(),), incs=(sem_a.inc(),))
    lb.op("vector", lambda e: e.tensor_scalar_mul(a_s[:], a_s[:], -1.0), waits=(sem_a.at(),), incs=(sem_a.inc(),))

    wx_s = lb.sb("wx_s", [128, NDT, 96], BF16)
    wdt_s = lb.sb("wdt_s", [128, DI], BF16)
    sem_w = lb.sem("wts")
    lb.op("gpsimd", lambda e: e.dma_start(out=wx_s[:], in_=wx.rearrange("(a p) o -> p a o", p=128)),
          incs=(sem_w.inc(16),))
    lb.op("gpsimd", lambda e: e.dma_start(out=wdt_s[0:64, :], in_=wdt), incs=(sem_w.inc(16),))
    win_s = lb.sb("win_s", [128, NCT, DI], BF16)
    wo_s = [lb.sb(f"wo_s{i}", [128, NDT, 128], BF16) for i in range(2)]

    h2T = lb.sb("h2T", [128, NCT, CHUNK], BF16)
    xsT = lb.sb("xsT", [128, NDT, CHUNK], BF16)
    dp_s = lb.sb("dp_s", [128, CHUNK], BF16)
    bs_s = lb.sb("bs_s", [128, CHUNK], BF16)
    cs_s = lb.sb("cs_s", [128, CHUNK], BF16)
    bbc = lb.sb("bbc", [128, NST, CHUNK], BF16)
    cbc = lb.sb("cbc", [128, NST, CHUNK], BF16)
    hcb = win_s[:, 0:4, :].rearrange("p a (b t) -> p (a b) t", t=CHUNK)
    y2T = lb.sb("y2T", [128, NDT, CHUNK], BF16)
    outT = lb.sb("outT", [128, NCT, CHUNK])
    st_s = lb.sb("st_s", [128, NDT, NST])
    zt_t = lb.sb("zt_t", [128, CHUNK]); az_t = lb.sb("az_t", [128, CHUNK])
    delta_t = [lb.sb(f"delta_t{i}", [128, CHUNK]) for i in range(2)]
    dx_bf = [lb.sb(f"dx_bf{i}", [128, CHUNK], BF16) for i in range(2)]
    u_bf = lb.sb("u_bf", [128, CHUNK], BF16)
    hb_r = [lb.sb(f"hb_r{i}", [128, CHUNK], BF16) for i in range(4)]
    da_t = [lb.sb(f"da_t{i}", [128, CHUNK], BF16) for i in range(4)]
    szT = lb.sb("szT", [128, NDT, CHUNK], BF16)
    rl_t = lb.sb("rl_t", [128, CHUNK])
    y_t = dmp[:, 0:CHUNK]
    t1_t = dmp[:, CHUNK:2 * CHUNK]

    ps_b = [lb.ps(f"ps_b{i}", [128, 512]) for i in range(2)]
    ps_a = lb.ps("ps_a", [128, 512])
    ps_c = lb.ps("ps_c", [128, 512])
    ps_d = lb.ps("ps_d", [128, 512])
    ps_e = lb.ps("ps_e", [128, 512])
    ps_fp = [lb.ps(f"ps_fp{i}", [128, 512]) for i in range(2)]
    ps_f = ps_fp[0]
    ps_o = ps_fp[1]

    ld = lb.sem("ld"); ac = lb.sem("ac"); dv = lb.sem("dv")
    pe = lb.sem("pe"); ev = lb.sem("ev"); st = lb.sem("st")
    mmo = lb.sem("mmo"); mme = lb.sem("mme"); ldw = lb.sem("ldw")
    scn = lb.sem("scn"); psn = lb.sem("psn"); bce = lb.sem("bce"); wos = lb.sem("wos")
    ldS = [lb.sem("ldS0"), lb.sem("ldS1")]
    ldwO = [lb.sem("ldwO0"), lb.sem("ldwO1")]
    lb._scn_hist = []; lb._scan_hist = []; lb._hc_hist = []
    lb._sp_hist = []; lb._da_hist = []; lb._dx_hist = []

    lb.op("vector", lambda e: e.memset(st_s[:], 0.0), incs=(dv.inc(),))

    for c in range(NCHUNK):
        if c > 0:
            lb.barrier()
        lb.op("gpsimd", lambda e: e.dma_start(
            out=win_s[:], in_=win.rearrange("(a p) o -> p a o", p=128)[:, :, 0:DI]),
            incs=(ldw.inc(16),))
        for tj in range(TPC):
            ti = c * TPC + tj
            bi = tj % 2
            wfree = (dv.at(), ac.at()) if tj >= 2 else ()
            lb.op("sync", lambda e, ti=ti, bi=bi: e.dma_start(out=xin[bi][:], in_=x2[ti * 128:(ti + 1) * 128, :]),
                  waits=wfree, incs=(ldS[bi].inc(16),))
            emit_ln(lb, tmp, xin[bi], h_t, dmp, ac, dv,
                    wait_in=(ldS[bi].at(),) + ((pe.at(),) if (c > 0 or tj > 0) else ()))
            emit_transpose8(lb, h_t, lambda half, tj=tj: h2T[:, half * 4:(half + 1) * 4, tj * 128:(tj + 1) * 128],
                            ps_b, idf, pe, ev, (dv.at(), sem_c.at()))
        mme_xs = []
        for dt in range(NDT):
            pxs = ps_a if dt % 2 == 0 else ps_fp[0]
            if dt == 0:
                w = (ev.at(), ldw.at(), mme.at())
            elif dt >= 2:
                w = ((mme.h, mme_xs[dt - 2]),)
            else:
                w = ()
            for ct in range(NCT):
                lb.op("tensor", lambda e, dt=dt, ct=ct, pxs=pxs: e.matmul(
                    pxs[:], win_s[:, ct, dt * 128:(dt + 1) * 128], h2T[:, ct, :],
                    start=(ct == 0), stop=(ct == NCT - 1)),
                    waits=w if ct == 0 else (), incs=(mmo.inc(),) if ct == NCT - 1 else ())
            lb.op("scalar", lambda e, dt=dt, pxs=pxs: e.activation(xsT[:, dt, :], pxs[:], AF.Silu,
                                                          bias=binc[:, dt:dt + 1]),
                  waits=(mmo.at(),), incs=(mme.inc(),))
            mme_xs.append(mme.n)
        for (psd, colr, outb, bxcol) in ((ps_c, (0, 64), dp_s, 0), (ps_d, (64, 80), bs_s, 1),
                                         (ps_e, (80, 96), cs_s, 2)):
            np_ = colr[1] - colr[0]
            for dt in range(NDT):
                w = (mme.at(), sem_w.at(), bce.at()) if dt == 0 else ()
                lb.op("tensor", lambda e, dt=dt, psd=psd, colr=colr, np_=np_: e.matmul(
                    psd[0:np_, :], wx_s[:, dt, colr[0]:colr[1]], xsT[:, dt, :],
                    start=(dt == 0), stop=(dt == NDT - 1)),
                    waits=w, incs=(mmo.inc(),) if dt == NDT - 1 else ())
            lb.op("scalar", lambda e, psd=psd, outb=outb, np_=np_, bxcol=bxcol: e.activation(
                outb[0:np_, :], psd[0:np_, :], AF.Identity, bias=bx_s[0:np_, bxcol:bxcol + 1]),
                waits=(mmo.at(),), incs=(mme.inc(),))
        for n in range(NST):
            w = (mme.at(), bce.at()) if n == 0 else (mme.at(),)
            lb.op("tensor", lambda e, n=n: e.matmul(
                ps_f[:], sel_s[:, n * 128:(n + 1) * 128], bs_s[0:16, :], start=True, stop=True),
                waits=w, incs=(mmo.inc(),))
            lb.op("scalar", lambda e, n=n: e.copy(bbc[:, n, :], ps_f[:]), waits=(mmo.at(),), incs=(mme.inc(),))
            lb.op("tensor", lambda e, n=n: e.matmul(
                ps_f[:], sel_s[:, n * 128:(n + 1) * 128], cs_s[0:16, :], start=True, stop=True),
                waits=(mme.at(),), incs=(mmo.inc(),))
            lb.op("scalar", lambda e, n=n: e.copy(cbc[:, n, :], ps_f[:]), waits=(mmo.at(),), incs=(mme.inc(),))
        lb.op("scalar", lambda e: e.engine_nop() if False else e.copy(az_t[:, 0:1], bbc[:, 0, 0:1]),
              incs=(bce.inc(),))
        lb.op("gpsimd", lambda e: e.dma_start(
            out=win_s[:], in_=win.rearrange("(a p) o -> p a o", p=128)[:, :, DI:2 * DI]),
            waits=(mme.at(),), incs=(ldw.inc(16),))
        for dt in range(NDT):
            for ct in range(NCT):
                w = (ldw.at(), mme.at()) if ct == 0 else ()
                lb.op("tensor", lambda e, dt=dt, ct=ct: e.matmul(
                    ps_a[:], win_s[:, ct, dt * 128:(dt + 1) * 128], h2T[:, ct, :],
                    start=(ct == 0), stop=(ct == NCT - 1)),
                    waits=w, incs=(mmo.inc(),) if ct == NCT - 1 else ())
            lb.op("scalar", lambda e, dt=dt: e.activation(szT[:, dt, :], ps_a[:], AF.Silu,
                                                          bias=binc[:, 16 + dt:17 + dt]),
                  waits=(mmo.at(),), incs=(mme.inc(),))
        for dt in range(NDT):
            par = dt % 2
            if dt == 0:
                w5a = (mme.at(), lb._scn_hist[-1] if False else scn.at())
            elif dt >= 2:
                w5a = ((ac.h, lb._sp_hist[-2]),)
            else:
                w5a = ()
            lb.op("tensor", lambda e, dt=dt, par=par: e.matmul(
                ps_fp[par][:], wdt_s[0:64, dt * 128:(dt + 1) * 128], dp_s[0:64, :], start=True, stop=True),
                waits=w5a, incs=(mmo.inc(),))
            wzt = (mmo.at(),) + (((dv.h, lb._dx_hist[-1]),) if len(lb._dx_hist) >= 1 else ())
            lb.op("scalar", lambda e, dt=dt, par=par: e.activation(zt_t[:], ps_fp[par][:], AF.Identity,
                                                                   bias=bdtc[:, dt:dt + 1]), waits=wzt)
            lb.op("scalar", lambda e: e.activation(az_t[:], zt_t[:], AF.Abs))
            lb.op("scalar", lambda e: e.activation(az_t[:], az_t[:], AF.Exp, scale=-1.0))
            lb.op("scalar", lambda e: e.activation(az_t[:], az_t[:], AF.Ln, bias=1.0), incs=(ac.inc(),))
            lb._sp_hist.append(ac.n)
            wrl = (ac.at(),) + (((ac.h, lb._da_hist[-2]),) if len(lb._da_hist) >= 2 else ())
            lb.op("vector", lambda e: e.tensor_single_scalar(rl_t[:], zt_t[:], 0.0, ALU.max),
                  waits=wrl)
            lb.op("vector", lambda e, par=par: e.tensor_add(delta_t[par][:], az_t[:], rl_t[:]))
            lb.op("vector", lambda e, dt=dt, par=par: e.tensor_mul(dx_bf[par][:], delta_t[par][:], xsT[:, dt, :]),
                  incs=(dv.inc(),))
            lb._dx_hist.append(dv.n)
            for n in range(NST):
                bi = n % 4
                if len(lb._scan_hist) >= 4:
                    wda = ((scn.h, lb._scan_hist[-4]),)
                else:
                    wda = ()
                wdelta = ((dv.h, lb._dx_hist[-1]),) if n == 0 else ()
                if dt == 0 and n == 0:
                    wdelta = wdelta + (sem_a.at(),)
                lb.op("scalar", lambda e, dt=dt, n=n, bi=bi, par=par: e.activation(
                    da_t[bi][:], delta_t[par][:], AF.Exp, scale=a_s[:, dt, n:n + 1]),
                    waits=wda + wdelta, incs=(ac.inc(),))
                if n == 15:
                    lb._da_hist.append(ac.n)
                hb = hb_r[n % 4]
                lb.op("vector", lambda e, n=n, par=par: e.tensor_mul(u_bf[:], dx_bf[par][:], bbc[:, n, :]))
                lb.op("vector", lambda e, dt=dt, n=n, bi=bi, hb=hb: e.tensor_tensor_scan(
                    hb[:], da_t[bi][:], u_bf[:], st_s[:, dt, n:n + 1], ALU.mult, ALU.add),
                    waits=(ac.at(),), incs=(scn.inc(),))
                lb._scan_hist.append(scn.n)
                lb.op("vector", lambda e, n=n, hb=hb: e.tensor_mul(hcb[:, n, :], hb[:], cbc[:, n, :]),
                      incs=(psn.inc(),))
                lb._hc_hist.append(psn.n)
                if n >= 1:
                    lb.op("vector", lambda e, dt=dt, n=n: e.tensor_copy(
                        st_s[:, dt, n - 1:n], hb_r[(n - 1) % 4][:, CHUNK - 1:CHUNK]))
            lb.op("vector", lambda e, dt=dt: e.tensor_copy(st_s[:, dt, 15:16], hb_r[15 % 4][:, CHUNK - 1:CHUNK]))
            red = outT[:]
            lb.op("vector", lambda e, red=red: e.tensor_tensor(
                red[:, 0:8, :], hcb[:, 0:8, :], hcb[:, 8:16, :], ALU.add))
            lb.op("vector", lambda e, red=red: e.tensor_tensor(
                red[:, 0:4, :], red[:, 0:4, :], red[:, 4:8, :], ALU.add))
            lb.op("vector", lambda e, red=red: e.tensor_tensor(
                red[:, 0:2, :], red[:, 0:2, :], red[:, 2:4, :], ALU.add))
            lb.op("vector", lambda e, red=red: e.tensor_tensor(
                y_t[:], red[:, 0, :], red[:, 1, :], ALU.add))
            lb.op("vector", lambda e, dt=dt: e.scalar_tensor_tensor(
                t1_t[:], xsT[:, dt, :], dpc[:, dt:dt + 1], y_t[:], ALU.mult, ALU.add))
            lb.op("vector", lambda e, dt=dt: e.tensor_mul(y2T[:, dt, :], t1_t[:], szT[:, dt, :]),
                  incs=(dv.inc(),))
        mme_after_evac = []
        for o in range(NCT):
            bi = o % 2
            w_buf = ((mme.h, mme_after_evac[o - 2]),) if o >= 2 else ()
            lb.op("gpsimd", lambda e, o=o, bi=bi: e.dma_start(
                out=wo_s[bi][:], in_=wout.rearrange("(a p) o -> p a o", p=128)[:, :, o * 128:(o + 1) * 128]),
                waits=w_buf, incs=(ldwO[bi].inc(16),))
            for dt in range(NDT):
                w = (dv.at(), ldwO[bi].at(), mme.at()) if dt == 0 else ()
                lb.op("tensor", lambda e, o=o, dt=dt, bi=bi: e.matmul(
                    ps_c[:], wo_s[bi][:, dt, :], y2T[:, dt, :],
                    start=(dt == 0), stop=(dt == NDT - 1)),
                    waits=w, incs=(mmo.inc(),) if dt == NDT - 1 else ())
            lb.op("scalar", lambda e, o=o: e.activation(outT[:, o, :], ps_c[:], AF.Identity,
                                                        bias=boutc[:, o:o + 1]),
                  waits=(mmo.at(),), incs=(mme.inc(),))
            mme_after_evac.append(mme.n)
        for tj in range(TPC):
            ti = c * TPC + tj
            bi = tj % 2
            lb.op("sync", lambda e, ti=ti, bi=bi: e.dma_start(out=xin[bi][:], in_=x2[ti * 128:(ti + 1) * 128, :]),
                  waits=(st.at(),) if ti >= 2 else (), incs=(ldS[bi].inc(16),))
            for half in range(2):
                pbuf = ps_b[half]
                for q in range(4):
                    ct = half * 4 + q
                    w = (mme.at(), st.at(), ev.at()) if q == 0 else ()
                    lb.op("tensor", lambda e, tj=tj, ct=ct, q=q, pbuf=pbuf: e.transpose(
                        pbuf[:, q * 128:(q + 1) * 128], outT[:, ct, tj * 128:(tj + 1) * 128], idf[:]),
                        waits=w, incs=(pe.inc(),) if q == 3 else ())
                lb.op("scalar", lambda e, half=half, pbuf=pbuf: e.copy(
                    out_tok[:, half * 512:(half + 1) * 512], pbuf[:]),
                    waits=(pe.at(),), incs=(ev.inc(),))
            lb.op("vector", lambda e, bi=bi: e.tensor_add(out_tok[:], xin[bi][:], out_tok[:]),
                  waits=(ev.at(), ldS[bi].at()), incs=(dv.inc(),))
            lb.op("sync", lambda e, ti=ti: e.dma_start(out=x3[ti * 128:(ti + 1) * 128, :], in_=out_tok[:]),
                  waits=(dv.at(),), incs=(st.inc(16),))
    lb.op("sync", lambda e: e.nop(), waits=(st.at(),))
    lb.pop()


def decl_l1_attn(lb, x_ap=None, out_ap=None):
    ext = {}
    ext["x2"] = x_ap if x_ap is not None else lb.dram("x2", [T, C], F32, kind="ExternalInput").ap()
    ext["win"] = lb.dram("win", [C, 2 * DI], F32, kind="ExternalInput").ap()
    ext["bin"] = lb.dram("bin", [2 * DI], F32, kind="ExternalInput").ap()
    ext["wx"] = lb.dram("wx", [DI, 96], F32, kind="ExternalInput").ap()
    ext["bx"] = lb.dram("bx", [96], F32, kind="ExternalInput").ap()
    ext["wdt"] = lb.dram("wdt", [64, DI], F32, kind="ExternalInput").ap()
    ext["bdt"] = lb.dram("bdt", [DI], F32, kind="ExternalInput").ap()
    ext["alog"] = lb.dram("alog", [DI, NST], F32, kind="ExternalInput").ap()
    ext["dp_v"] = lb.dram("dp_v", [DI], F32, kind="ExternalInput").ap()
    ext["wout"] = lb.dram("wout", [DI, C], F32, kind="ExternalInput").ap()
    ext["bout"] = lb.dram("bout", [C], F32, kind="ExternalInput").ap()
    ext["sel16"] = lb.dram("sel16", [16, 16 * 128], BF16, kind="ExternalInput").ap()
    ext["x3"] = out_ap if out_ap is not None else lb.dram("x3", [T, C], F32, kind="ExternalOutput").ap()
    return ext


def build_l1_attn():
    lb = LB()
    lb.ident_ap = lb.dram("i128f", [128, 128], F32, kind="ExternalInput").ap()
    lb.init_consts([1e-5, -0.5, -1.0, 1.0, 1.0 / 1024.0, LN8])
    ext = decl_l1_attn(lb)
    cm = make_common(lb)
    emit_l1_attn(lb, cm, ext)
    return lb.build()


def host_prep_l1attn(inp, x2_core):
    sel = np.zeros((16, 16 * 128), dtype=ml_dtypes.bfloat16)
    for n in range(16):
        sel[n, n * 128:(n + 1) * 128] = 1.0
    return {
        "x2": x2_core,
        "win": inp["L1_attn_Win"], "bin": inp["L1_attn_bin"],
        "wx": inp["L1_attn_Wx"], "bx": inp["L1_attn_bx"],
        "wdt": inp["L1_attn_Wdt"], "bdt": inp["L1_attn_bdt"],
        "alog": inp["L1_attn_A_log"], "dp_v": inp["L1_attn_Dp"],
        "wout": inp["L1_attn_Wout"], "bout": inp["L1_attn_bout"],
        "sel16": sel,
        "i128f": np.eye(128, dtype=np.float32),
    }


NE = 4
NI = 2752
NIT = 22
assert NI == 128 * 21 + 64

def emit_l1_ffn(lb, cm, ext):
    lb.pfx = "D_"
    x3 = ext["x3"]; gwT = ext["gwT"]; egT = ext["egT"]; euT = ext["euT"]
    edT = ext["edT"]; sgT = ext["sgT"]; suT = ext["suT"]; sdT = ext["sdT"]
    sel4 = ext["sel4"]; x4 = ext["x4"]; z_scr = ext["z_scr"]
    tmp = cm["tmp"]; idf = cm["idf"]
    xin = cm["xin"]; dmp = cm["dmp"]; h_t = cm["ha"]; out_tok = cm["otok"]
    lb.push()
    sem_c = cm["sem_c"]
    gw_s = lb.sb("gw_s", [128, NCT, NE], BF16)
    lb.op("gpsimd", lambda e: e.dma_start(out=gw_s[:], in_=gwT.rearrange("(a p) o -> p a o", p=128)),
          incs=(sem_c.inc(16),))
    lb.op("gpsimd", None, waits=(sem_c.at(),))
    sel_s = lb.sb("sel_s", [4, 4 * 128], BF16)
    lb.op("sync", lambda e: e.dma_start(out=sel_s[0:4, :], in_=sel4), incs=(sem_c.inc(16),))
    lb.op("sync", None, waits=(sem_c.at(),))

    spc = lb.sb("spc", [128, 256])

    ld = lb.sem("ld"); ac = lb.sem("ac"); dv = lb.sem("dv")
    pe = lb.sem("pe"); ev = lb.sem("ev"); st = lb.sem("st")
    mmo = lb.sem("mmo"); mme = lb.sem("mme"); ldw = lb.sem("ldw"); zst = lb.sem("zst")
    ldS = [lb.sem("ldS0"), lb.sem("ldS1")]
    ldw2 = [lb.sem("ldw20"), lb.sem("ldw21")]
    ldwE = [lb.sem("ldwE0"), lb.sem("ldwE1")]
    ldzE = [lb.sem(f"ldzE{i}") for i in range(5)]

    lb.op("vector", lambda e: e.memset(spc[:], 0.0), incs=(dv.inc(),))

    lb.push()
    h3T = lb.sb("h3T", [128, NCT, T], BF16)
    sc_all = lb.sb("sc_all", [128, NTILE * NE])
    ps_b = [lb.ps(f"ps_b{i}", [128, 512]) for i in range(2)]
    ps_gp = [lb.ps(f"ps_gp{i}", [128, 512]) for i in range(2)]
    ps_up = [lb.ps(f"ps_up{i}", [128, 512]) for i in range(2)]
    ps_g = ps_gp[0]
    ps_u = ps_up[0]

    sc3 = lambda: sc_all[:].rearrange("p (i e) -> p i e", e=NE)
    for i in range(NTILE):
        bi = i % 2
        wfree = (dv.at(), ac.at()) if i >= 2 else ()
        lb.op("sync", lambda e, i=i, bi=bi: e.dma_start(out=xin[bi][:], in_=x3[i * 128:(i + 1) * 128, :]),
              waits=wfree, incs=(ldS[bi].inc(16),))
        emit_ln(lb, tmp, xin[bi], h_t, dmp, ac, dv,
                wait_in=(ldS[bi].at(),) + ((pe.at(),) if i > 0 else ()))
        emit_transpose8(lb, h_t, lambda half, i=i: h3T[:, half * 4:(half + 1) * 4, i * 128:(i + 1) * 128],
                        ps_b, idf, pe, ev, (dv.at(), sem_c.at()))
        for ct in range(NCT):
            w = (ev.at(), mme.at()) if ct == 0 else ()
            lb.op("tensor", lambda e, i=i, ct=ct: e.matmul(
                ps_g[:, 0:NE], h3T[:, ct, i * 128:(i + 1) * 128], gw_s[:, ct, :],
                start=(ct == 0), stop=(ct == NCT - 1)),
                waits=w, incs=(mmo.inc(),) if ct == NCT - 1 else ())
        lb.op("scalar", lambda e, i=i: e.copy(sc3()[:, i, :], ps_g[:, 0:NE]),
              waits=(mmo.at(),), incs=(mme.inc(),))

    mx_t = lb.sb("mx_t", [128, NTILE]); ss_t = lb.sb("ss_t", [128, NTILE])
    m1_t = lb.sb("m1_t", [128, NTILE]); m2_t = lb.sb("m2_t", [128, NTILE])
    n_t = lb.sb("n_t", [128, NTILE]); t1g = lb.sb("t1g", [128, NTILE])
    e_t = lb.sb("e_t", [128, NTILE * NE]); p_t = lb.sb("p_t", [128, NTILE * NE])
    g1_t = lb.sb("g1_t", [128, NTILE * NE]); sel1 = lb.sb("sel1", [128, NTILE * NE])
    sel2 = lb.sb("sel2", [128, NTILE * NE]); p2_t = lb.sb("p2_t", [128, NTILE * NE])
    comb_t = lb.sb("comb_t", [128, NTILE * NE])
    combT = lb.sb("combT", [4, T], BF16)
    d_t = lb.sb("d_t", [128, NTILE])
    v4 = lambda ap: ap[:].rearrange("p (i e) -> p i e", e=NE)
    bc4 = lambda ap: ap[:].unsqueeze(2).broadcast_to([128, NTILE, NE])

    def spacer():
        lb.op("vector", lambda e: e.tensor_scalar_mul(spc[:], spc[:], 1.0))

    def first_occurrence(src, dst):
        lb.op("vector", lambda e: e.tensor_scalar(n_t[:], v4(src)[:, :, 0], -1.0, 1.0, ALU.mult, ALU.add))
        lb.op("vector", lambda e: e.tensor_scalar(t1g[:], v4(src)[:, :, 1], -1.0, 1.0, ALU.mult, ALU.add))
        lb.op("vector", lambda e: e.tensor_copy(v4(dst)[:, :, 0], v4(src)[:, :, 0]))
        lb.op("vector", lambda e: e.tensor_mul(v4(dst)[:, :, 1], v4(src)[:, :, 1], n_t[:]))
        lb.op("vector", lambda e: e.tensor_mul(n_t[:], n_t[:], t1g[:]))
        lb.op("vector", lambda e: e.tensor_scalar(t1g[:], v4(src)[:, :, 2], -1.0, 1.0, ALU.mult, ALU.add))
        lb.op("vector", lambda e: e.tensor_mul(v4(dst)[:, :, 2], v4(src)[:, :, 2], n_t[:]))
        lb.op("vector", lambda e: e.tensor_mul(n_t[:], n_t[:], t1g[:]))
        spacer()
        lb.op("vector", lambda e: e.tensor_mul(v4(dst)[:, :, 3], v4(src)[:, :, 3], n_t[:]))

    lb.op("vector", lambda e: e.reduce_max(mx_t[:], sc3(), axis=AX.X), waits=(mme.at(),))
    spacer()
    lb.op("vector", lambda e: e.tensor_sub(v4(e_t), sc3(), bc4(mx_t)), incs=(dv.inc(),))
    lb.op("scalar", lambda e: e.activation(e_t[:], e_t[:], AF.Exp), waits=(dv.at(),), incs=(ac.inc(),))
    lb.op("vector", lambda e: e.reduce_sum(ss_t[:], v4(e_t), axis=AX.X), waits=(ac.at(),))
    spacer()
    lb.op("vector", lambda e: e.reciprocal(ss_t[:], ss_t[:]))
    spacer()
    lb.op("vector", lambda e: e.tensor_mul(v4(p_t), v4(e_t), bc4(ss_t)))
    lb.op("vector", lambda e: e.reduce_max(m1_t[:], v4(p_t), axis=AX.X))
    spacer()
    lb.op("vector", lambda e: e.tensor_tensor(v4(g1_t), v4(p_t), bc4(m1_t), ALU.is_ge))
    first_occurrence(g1_t, sel1)
    lb.op("vector", lambda e: e.scalar_tensor_tensor(p2_t[:], sel1[:], -1e30, p_t[:], ALU.mult, ALU.add))
    lb.op("vector", lambda e: e.reduce_max(m2_t[:], v4(p2_t), axis=AX.X))
    spacer()
    lb.op("vector", lambda e: e.tensor_tensor(v4(g1_t), v4(p2_t), bc4(m2_t), ALU.is_ge))
    first_occurrence(g1_t, sel2)
    lb.op("vector", lambda e: e.tensor_add(d_t[:], m1_t[:], m2_t[:]))
    spacer()
    lb.op("vector", lambda e: e.tensor_scalar(d_t[:], d_t[:], 1.0, 1e-20, ALU.mult, ALU.add))
    spacer()
    lb.op("vector", lambda e: e.reciprocal(d_t[:], d_t[:]))
    spacer()
    lb.op("vector", lambda e: e.tensor_mul(m1_t[:], m1_t[:], d_t[:]))
    lb.op("vector", lambda e: e.tensor_mul(m2_t[:], m2_t[:], d_t[:]))
    spacer()
    lb.op("vector", lambda e: e.tensor_mul(v4(comb_t), v4(sel1), bc4(m1_t)))
    lb.op("vector", lambda e: e.tensor_mul(v4(e_t), v4(sel2), bc4(m2_t)))
    lb.op("vector", lambda e: e.tensor_add(comb_t[:], comb_t[:], e_t[:]), incs=(dv.inc(),))
    for i in range(NTILE):
        lb.op("tensor", lambda e, i=i: e.transpose(
            ps_g[0:NE, 0:128], v4(comb_t)[:, i, :], idf[:]),
            waits=(dv.at(), mme.at()), incs=(mmo.inc(),))
        lb.op("scalar", lambda e, i=i: e.copy(combT[0:NE, i * 128:(i + 1) * 128], ps_g[0:NE, 0:128]),
              waits=(mmo.at(),), incs=(mme.inc(),))
    cbc_s = lb.sb("cbc_s", [128, NE, NCHUNK, CHUNK], BF16)
    for e_ in range(NE):
        for c in range(NCHUNK):
            lb.op("tensor", lambda e, e_=e_, c=c: e.matmul(
                ps_g[:], sel_s[0:4, e_ * 128:(e_ + 1) * 128], combT[0:4, c * CHUNK:(c + 1) * CHUNK],
                start=True, stop=True),
                waits=(mme.at(),), incs=(mmo.inc(),))
            lb.op("scalar", lambda e, e_=e_, c=c: e.copy(cbc_s[:, e_, c, :], ps_g[:]),
                  waits=(mmo.at(),), incs=(mme.inc(),))
    lb.barrier()

    wg_s = [lb.sb(f"wg_s{i}", [128, NCT, 128], BF16) for i in range(2)]
    wu_s = [lb.sb(f"wu_s{i}", [128, NCT, 128], BF16) for i in range(2)]
    sg_b = [lb.sb(f"sg_b{i}", [128, CHUNK], BF16) for i in range(2)]
    u2_b = [lb.sb(f"u2_b{i}", [128, CHUNK], BF16) for i in range(2)]
    t2_t = lb.sb("t2_t", [128, CHUNK], BF16)
    z_bf = [lb.sb(f"z_bf{i}", [128, CHUNK], BF16) for i in range(2)]
    mme_after = []
    dv_after = []
    zsem = [lb.sem(f"z{i}") for i in range(2)]
    it = 0
    for e_ in range(5):
        for i in range(NIT):
            bi = it % 2
            rows = 128 if i < 21 else 64
            w_buf = ((mme.h, mme_after[it - 2]),) if it >= 2 else ()
            for (wsl, src) in ((wg_s, egT if e_ < 4 else sgT), (wu_s, euT if e_ < 4 else suT)):
                lb.op("gpsimd", lambda e, e_=e_, i=i, bi=bi, wsl=wsl, src=src, rows=rows: e.dma_start(
                    out=wsl[bi][:, :, 0:rows],
                    in_=(src[e_] if e_ < 4 else src).rearrange("(a p) o -> p a o", p=128)[:, :, i * 128:i * 128 + rows]),
                    waits=w_buf, incs=(ldw2[bi].inc(16),))
                w_buf = ()
            mme_c = []
            dv_c = []
            for c in range(NCHUNK):
                par = c % 2
                psg = ps_gp[par]; psu = ps_up[par]
                if c >= 2:
                    w = (ldw2[bi].at(), (mme.h, mme_c[c - 2]))
                elif it >= 1:
                    w = (ldw2[bi].at(), (mme.h, mme_after[it - 1]))
                else:
                    w = (ldw2[bi].at(),)
                for ct in range(NCT):
                    lb.op("tensor", lambda e, bi=bi, ct=ct, c=c, rows=rows, psg=psg: e.matmul(
                        psg[0:rows, :], wg_s[bi][:, ct, 0:rows], h3T[:, ct, c * CHUNK:(c + 1) * CHUNK],
                        start=(ct == 0), stop=(ct == NCT - 1)), waits=w if ct == 0 else ())
                for ct in range(NCT):
                    lb.op("tensor", lambda e, bi=bi, ct=ct, c=c, rows=rows, psu=psu: e.matmul(
                        psu[0:rows, :], wu_s[bi][:, ct, 0:rows], h3T[:, ct, c * CHUNK:(c + 1) * CHUNK],
                        start=(ct == 0), stop=(ct == NCT - 1)),
                        incs=(mmo.inc(),) if ct == NCT - 1 else ())
                wact = (mmo.at(),)
                if c >= 2:
                    wact = wact + ((dv.h, dv_c[c - 2]),)
                elif it >= 1:
                    wact = wact + ((dv.h, dv_after[it - 1]),)
                lb.op("scalar", lambda e, rows=rows, psg=psg, par=par: e.activation(
                    sg_b[par][0:rows, :], psg[0:rows, :], AF.Silu), waits=wact)
                lb.op("scalar", lambda e, rows=rows, psu=psu, par=par: e.copy(
                    u2_b[par][0:rows, :], psu[0:rows, :]), incs=(mme.inc(),))
                mme_c.append(mme.n)
                zslot = c % 2
                lb.op("vector", lambda e, rows=rows, par=par: e.tensor_mul(
                    t2_t[0:rows, :], sg_b[par][0:rows, :], u2_b[par][0:rows, :]),
                    waits=(mme.at(), (zsem[zslot].h, zsem[zslot].n)))
                if e_ < 4:
                    lb.op("vector", lambda e, e_=e_, c=c, zslot=zslot, rows=rows: e.tensor_mul(
                        z_bf[zslot][0:rows, :], t2_t[0:rows, :], cbc_s[0:rows, e_, c, :]),
                        incs=(dv.inc(),))
                else:
                    lb.op("vector", lambda e, zslot=zslot, rows=rows: e.tensor_scalar_mul(
                        z_bf[zslot][0:rows, :], t2_t[0:rows, :], 1.0), incs=(dv.inc(),))
                dv_c.append(dv.n)
                lb.op("sync", lambda e, e_=e_, i=i, c=c, zslot=zslot, rows=rows: e.dma_start(
                    out=z_scr[e_, i, 0:rows, c * CHUNK:(c + 1) * CHUNK], in_=z_bf[zslot][0:rows, :]),
                    waits=(dv.at(),), incs=(zsem[zslot].inc(16),))
            mme_after.append(mme.n)
            dv_after.append(dv.n)
            it += 1
    lb.op("sync", lambda e: e.nop(), waits=(zsem[0].at(), zsem[1].at()))
    lb.barrier()
    lb.pop()

    lb.push()
    z_sb = lb.sb("z_sb", [128, 5 * NIT, CHUNK], BF16)
    ed_sb = [lb.sb(f"ed_sb{i}", [128, 5 * NIT, 128], BF16) for i in range(2)]
    outT = lb.sb("outT", [128, NCT, CHUNK])
    ps_y = lb.ps("ps_y", [128, 512])
    ps_t = [lb.ps(f"ps_t{i}", [128, 512]) for i in range(2)]
    ldz = lb.sem("ldz")
    mme_after_o = []
    it = 0
    for c in range(NCHUNK):
        if c > 0:
            lb.barrier()
        ldz_hist = []
        for e_ in range(5):
            src = edT if e_ < 4 else sdT
            lb.op("sync", lambda e, e_=e_, c=c, src=src: e.dma_start(
                out=z_sb[:, e_ * NIT:(e_ + 1) * NIT, :],
                in_=z_scr[e_].rearrange("i p t -> p i t")[:, :, c * CHUNK:(c + 1) * CHUNK]),
                incs=(ldzE[e_].inc(16),))
            ldz_hist.append((ldzE[e_].h, ldzE[e_].n))
        for o in range(NCT):
            bi = it % 2
            w_buf = ((mme.h, mme_after_o[it - 2]),) if it >= 2 else ()
            for e_ in range(5):
                src = edT[e_] if e_ < 4 else sdT
                lb.op("gpsimd", lambda e, o=o, bi=bi, src=src, e_=e_: e.dma_start(
                    out=ed_sb[bi][:, e_ * NIT:e_ * NIT + 21, :],
                    in_=src[0:2688, :].rearrange("(a p) o -> p a o", p=128)[:, :, o * 128:(o + 1) * 128]),
                    waits=w_buf, incs=(ldwE[bi].inc(16),))
                w_buf = ()
                lb.op("gpsimd", lambda e, o=o, bi=bi, src=src, e_=e_: e.dma_start(
                    out=ed_sb[bi][0:64, e_ * NIT + 21, :],
                    in_=src[2688:2752, o * 128:(o + 1) * 128]),
                    incs=(ldwE[bi].inc(16),))
            for k in range(5 * NIT):
                e_i, i_i = divmod(k, NIT)
                rows = 128 if i_i < 21 else 64
                w = ()
                if k == 0:
                    w = (ldz_hist[0], ldwE[bi].at(), mme.at())
                elif i_i == 0:
                    w = (ldz_hist[e_i],)
                lb.op("tensor", lambda e, k=k, bi=bi, rows=rows: e.matmul(
                    ps_y[:], ed_sb[bi][0:rows, k, :], z_sb[0:rows, k, :],
                    start=(k == 0), stop=(k == 5 * NIT - 1)),
                    waits=w, incs=(mmo.inc(),) if k == 5 * NIT - 1 else ())
            lb.op("scalar", lambda e, o=o: e.copy(outT[:, o, :], ps_y[:]),
                  waits=(mmo.at(),), incs=(mme.inc(),))
            mme_after_o.append(mme.n)
            it += 1
        for tj in range(TPC):
            ti = c * TPC + tj
            bi = tj % 2
            lb.op("sync", lambda e, ti=ti, bi=bi: e.dma_start(out=xin[bi][:], in_=x3[ti * 128:(ti + 1) * 128, :]),
                  waits=(st.at(),) if ti >= 2 else (), incs=(ldS[bi].inc(16),))
            for half in range(2):
                pbuf = ps_t[half]
                for q in range(4):
                    ct = half * 4 + q
                    w = (mme.at(), st.at(), ev.at()) if q == 0 else ()
                    lb.op("tensor", lambda e, tj=tj, ct=ct, q=q, pbuf=pbuf: e.transpose(
                        pbuf[:, q * 128:(q + 1) * 128], outT[:, ct, tj * 128:(tj + 1) * 128], idf[:]),
                        waits=w, incs=(pe.inc(),) if q == 3 else ())
                lb.op("scalar", lambda e, half=half, pbuf=pbuf: e.copy(
                    out_tok[:, half * 512:(half + 1) * 512], pbuf[:]),
                    waits=(pe.at(),), incs=(ev.inc(),))
            lb.op("vector", lambda e, bi=bi: e.tensor_add(out_tok[:], xin[bi][:], out_tok[:]),
                  waits=(ev.at(), ldS[bi].at()), incs=(dv.inc(),))
            lb.op("sync", lambda e, ti=ti: e.dma_start(out=x4[ti * 128:(ti + 1) * 128, :], in_=out_tok[:]),
                  waits=(dv.at(),), incs=(st.inc(16),))
    lb.op("sync", lambda e: e.nop(), waits=(st.at(),))
    lb.pop()
    lb.pop()


def decl_l1_ffn(lb, x_ap=None, out_ap=None):
    ext = {}
    ext["x3"] = x_ap if x_ap is not None else lb.dram("x3", [T, C], F32, kind="ExternalInput").ap()
    ext["gwT"] = lb.dram("gwT", [C, NE], F32, kind="ExternalInput").ap()
    ext["egT"] = lb.dram("egT", [NE, C, NI], F32, kind="ExternalInput").ap()
    ext["euT"] = lb.dram("euT", [NE, C, NI], F32, kind="ExternalInput").ap()
    ext["edT"] = lb.dram("edT", [NE, NI, C], F32, kind="ExternalInput").ap()
    ext["sgT"] = lb.dram("sgT", [C, NI], F32, kind="ExternalInput").ap()
    ext["suT"] = lb.dram("suT", [C, NI], F32, kind="ExternalInput").ap()
    ext["sdT"] = lb.dram("sdT", [NI, C], F32, kind="ExternalInput").ap()
    ext["sel4"] = lb.dram("sel4", [4, 4 * 128], BF16, kind="ExternalInput").ap()
    ext["x4"] = out_ap if out_ap is not None else lb.dram("x4", [T, C], F32, kind="ExternalOutput").ap()
    ext["z_scr"] = lb.dram("z_scr", [5, NIT, 128, T], BF16, kind="Internal").ap()
    return ext


def build_l1_ffn():
    lb = LB()
    lb.ident_ap = lb.dram("i128f", [128, 128], F32, kind="ExternalInput").ap()
    lb.init_consts([1e-5, -0.5, -1.0, 1.0, 1.0 / 1024.0, LN8])
    ext = decl_l1_ffn(lb)
    cm = make_common(lb)
    emit_l1_ffn(lb, cm, ext)
    return lb.build()


def host_prep_l1ffn(inp, x3_core):
    ct = np.ascontiguousarray
    sel = np.zeros((4, 4 * 128), dtype=ml_dtypes.bfloat16)
    for n in range(4):
        sel[n, n * 128:(n + 1) * 128] = 1.0
    return {
        "x3": x3_core,
        "gwT": ct(inp["L1_ffn_gate_w"].T),
        "egT": ct(inp["L1_ffn_eg"].transpose(0, 2, 1)),
        "euT": ct(inp["L1_ffn_eu"].transpose(0, 2, 1)),
        "edT": ct(inp["L1_ffn_ed"].transpose(0, 2, 1)),
        "sgT": ct(inp["L1_ffn_sg"].T), "suT": ct(inp["L1_ffn_su"].T),
        "sdT": ct(inp["L1_ffn_sd"].T),
        "sel4": sel,
        "i128f": np.eye(128, dtype=np.float32),
    }



def build_fused(debug=False):
    lb = LB()
    kind = "ExternalOutput" if debug else "Internal"
    lb.ident_ap = lb.dram("i128f", [128, 128], F32, kind="ExternalInput").ap()
    lb.init_consts([1e-5, -0.5, -1.0, 1.0, 1.0 / 1024.0, LN8])
    x1 = lb.dram("x1", [T, C], F32, kind=kind).ap()
    x2 = lb.dram("x2", [T, C], F32, kind=kind).ap()
    x3 = lb.dram("x3", [T, C], F32, kind=kind).ap()
    extA = decl_l0_attn(lb, out_ap=x1)
    extB = decl_l0_ffn(lb, x_ap=x1, out_ap=x2)
    extC = decl_l1_attn(lb, x_ap=x2, out_ap=x3)
    extD = decl_l1_ffn(lb, x_ap=x3)
    cm = make_common(lb)
    emit_l0_attn(lb, cm, extA)
    lb.barrier()
    emit_l0_ffn(lb, cm, extB)
    lb.barrier()
    emit_l1_attn(lb, cm, extC)
    lb.barrier()
    emit_l1_ffn(lb, cm, extD)
    return lb.build()


def host_prep_fused(inp, x_core):
    m = {}
    m.update(host_prep_l0attn(inp, x_core))
    m.update(host_prep_l0ffn(inp, None)); m.pop("x1")
    m.update(host_prep_l1attn(inp, None)); m.pop("x2")
    m.update(host_prep_l1ffn(inp, None)); m.pop("x3")
    m["x0"] = x_core
    return m


def kernel(**inputs):
    x = np.ascontiguousarray(np.asarray(inputs["L0_attn_x"], dtype=np.float32))
    assert x.shape == (8, T, C)
    nc = build_fused()
    prep = host_prep_fused(inputs, x[0])
    maps = [{**prep, "x0": np.ascontiguousarray(x[c])} for c in range(8)]
    res = run_bass_kernel_spmd(nc, maps, core_ids=list(range(8)))
    return np.stack([np.asarray(r["x4"]) for r in res.results], axis=0)


# version 13 (final)
# speedup vs baseline: 1.0395x; 1.0395x over previous
import numpy as np
import ml_dtypes
from contextlib import ExitStack
from collections import defaultdict

import concourse.bass as bass
import concourse.mybir as mybir
from concourse.bass_utils import run_bass_kernel_spmd

F32 = mybir.dt.float32
BF16 = mybir.dt.bfloat16
U8 = mybir.dt.uint8
AF = mybir.ActivationFunctionType
ALU = mybir.AluOpType
AX = mybir.AxisListType

T = 4096
C = 1024
NTILE = T // 128
NCT = C // 128
CHUNK = 512
NCHUNK = T // CHUNK
TPC = CHUNK // 128
MAGIC = 12582912.0

ENGINES = ("sync", "gpsimd", "tensor", "scalar", "vector")


class SemCtr:
    def __init__(self, handle):
        self.h = handle
        self.n = 0

    def inc(self, amt=1):
        self.n += amt
        return (self.h, amt)

    def at(self):
        return (self.h, self.n)


class LB:
    def __init__(self):
        self.nc = bass.Bass("TRN2", target_bir_lowering=False)
        self.es = ExitStack()
        self.scopes = []
        self.seq = defaultdict(list)
        self.sem_i = 0
        self.ten_i = 0
        self.pfx = ""

    def sem(self, name=None):
        h = self.es.enter_context(self.nc.semaphore(self.pfx + (name or f"s{self.sem_i}")))
        self.sem_i += 1
        return SemCtr(h)

    def _stack(self):
        return self.scopes[-1] if self.scopes else self.es

    def sb(self, name, shape, dtype=F32):
        return self._stack().enter_context(self.nc.sbuf_tensor(self.pfx + name, shape, dtype))

    def ps(self, name, shape, dtype=F32):
        return self._stack().enter_context(self.nc.psum_tensor(self.pfx + name, shape, dtype))

    def push(self):
        self.scopes.append(ExitStack())

    def pop(self):
        self.scopes.pop().close()

    def dram(self, name, shape, dtype=F32, kind="Internal"):
        return self.nc.dram_tensor(name, shape, dtype, kind=kind)

    def init_consts(self, values):
        cp = self.sb("constpool", [128, len(values)])
        self.const_sem = self.sem("constsem")
        for j, v in enumerate(values):
            self.op("vector", lambda e, j=j, v=v: e.memset(cp[:, j:j + 1], float(v)),
                    incs=(self.const_sem.inc(),) if j == len(values) - 1 else ())
            self.nc.const_aps.register(F32, float(v), cp[:, j:j + 1])
        for e in ("sync", "gpsimd", "tensor", "scalar"):
            self.op(e, None, waits=(self.const_sem.at(),))

    def op(self, engine, fn, waits=(), incs=()):
        self.seq[engine].append((tuple(waits), fn, tuple(incs)))

    def barrier(self, engines=ENGINES):
        if not hasattr(self, "_bar"):
            pfx = self.pfx
            self.pfx = ""
            self._bar = self.sem("barrier")
            self.pfx = pfx
        bar = self._bar
        for e in engines:
            self.op(e, lambda eng, b=bar: eng.sem_inc(b.h, 1))
        bar.n += len(engines)
        tgt = bar.at()
        for e in engines:
            self.op(e, None, waits=(tgt,))

    def build(self):
        assert not self.scopes
        with self.es:
            with self.nc.Block() as block:
                def mk(eng_name):
                    items = self.seq[eng_name]
                    def run(eng):
                        for waits, fn, incs in items:
                            for (h, v) in waits:
                                eng.wait_ge(h, v)
                            if fn is None:
                                continue
                            inst = fn(eng)
                            for (h, amt) in incs:
                                assert inst is not None
                                inst = inst.then_inc(h, amt)
                    return run
                block.sync(mk("sync"))
                block.gpsimd(mk("gpsimd"))
                block.tensor(mk("tensor"))
                block.scalar(mk("scalar"))
                block.vector(mk("vector"))
        return self.nc



def emit_vec_cols(lb, name, dram_ap, sem, n=C):
    t = lb.sb(name, [128, n // 128])
    lb.op("sync", lambda e, d=dram_ap, t=t: e.dma_start(
        out=t[:], in_=d.rearrange("(a p) -> p a", p=128), allow_slow_non_contiguous=True),
        incs=(sem.inc(16),))
    lb.op("sync", None, waits=(sem.at(),))
    return t


def emit_bcast_row(lb, name, dram_ap, sem, n=C):
    t = lb.sb(name, [128, n])
    lb.op("sync", lambda e, d=dram_ap, t=t: e.dma_start(
        out=t[:], in_=d.rearrange("(one f) -> one f", one=1).partition_broadcast(128)),
        incs=(sem.inc(16),))
    lb.op("sync", None, waits=(sem.at(),))
    return t


class LNTmp:
    def __init__(self, lb):
        self.s = lb.sb("ln_s", [128, 1])
        self.q = lb.sb("ln_q", [128, 1])
        self.v = lb.sb("ln_v", [128, 1])
        self.m = lb.sb("ln_m", [128, 1])
        self.r = lb.sb("ln_r", [128, 1])


def emit_ln(lb, tmp, xin, h_out, dmp, ac, dv, wait_in=(), d=C, eps=1e-5):
    lb.op("vector", lambda e: e.tensor_scalar(dmp[:], xin[:], 1.0, 0.0, ALU.mult, ALU.add,
                                              accum_out=tmp.s[:]), waits=wait_in, incs=(dv.inc(),))
    lb.op("vector", lambda e: e.scalar_tensor_tensor(dmp[:], xin[:], 1.0, xin[:], ALU.mult, ALU.mult,
                                                     accum_out=tmp.q[:]), incs=(dv.inc(),))
    lb.op("scalar", lambda e: e.activation(tmp.m[:], tmp.s[:], AF.Identity, scale=1.0 / d),
          waits=(dv.at(),))
    lb.op("scalar", lambda e: e.activation(tmp.v[:], tmp.m[:], AF.Square), incs=(ac.inc(),))
    lb.op("vector", lambda e: e.tensor_scalar(tmp.s[:], tmp.q[:], 1.0 / d, tmp.v[:],
                                              ALU.mult, ALU.subtract), waits=(ac.at(),), incs=(dv.inc(),))
    lb.op("scalar", lambda e: e.activation(tmp.v[:], tmp.s[:], AF.Ln, bias=eps), waits=(dv.at(),))
    lb.op("scalar", lambda e: e.activation(tmp.r[:], tmp.v[:], AF.Exp, scale=-0.5), incs=(ac.inc(),))
    lb.op("vector", lambda e: e.tensor_scalar(h_out[:], xin[:], tmp.m[:], tmp.r[:], ALU.subtract, ALU.mult),
          waits=(ac.at(),), incs=(dv.inc(),))


def emit_transpose8(lb, src_f32, dst_cb, ps_pair, idf, pe, ev, dv_wait, dst_dtype_scalar_copy=True):
    for half in range(2):
        pbuf = ps_pair[half]
        for q in range(4):
            ct = half * 4 + q
            w = ()
            if q == 0:
                w = dv_wait + (ev.at(),)
            lb.op("tensor", lambda e, ct=ct, q=q, pbuf=pbuf: e.transpose(
                pbuf[:, q * 128:(q + 1) * 128], src_f32[:, ct * 128:(ct + 1) * 128], idf[:]),
                waits=w, incs=(pe.inc(),) if q == 3 else ())
        lb.op("scalar", lambda e, half=half, pbuf=pbuf: e.copy(dst_cb(half), pbuf[:]),
              waits=(pe.at(),), incs=(ev.inc(),))


def make_common(lb):
    cm = {}
    cm["tmp"] = LNTmp(lb)
    cm["idf"] = lb.sb("cm_idf", [128, 128], F32)
    cm["sem_c"] = lb.sem("cm_c")
    lb.op("sync", lambda e, t=cm["idf"]: e.dma_start(out=t[:], in_=lb.ident_ap), incs=(cm["sem_c"].inc(16),))
    lb.op("sync", None, waits=(cm["sem_c"].at(),))
    cm["xin"] = [lb.sb(f"cm_xin{i}", [128, C]) for i in range(2)]
    cm["dmp"] = lb.sb("cm_dmp", [128, C])
    cm["ha"] = lb.sb("cm_ha", [128, C])
    cm["hb"] = lb.sb("cm_hb", [128, C])
    cm["otok"] = lb.sb("cm_otok", [128, C])
    return cm



def emit_l0_attn(lb, cm, ext, debug=False):
    lb.pfx = "A_"
    x0 = ext["x0"]; wkT = ext["wkT"]; wvT = ext["wvT"]; wrT = ext["wrT"]
    woT = ext["woT"]; wshT = ext["wshT"]; fwkT = ext["fwkT"]; fwvT = ext["fwvT"]
    fwrT = ext["fwrT"]; vecs = ext["vecs"]; x1 = ext["x1"]
    xi_scr = ext["xi_scr"]; kk_scr = ext["kk_scr"]; srr_scr = ext["srr_scr"]
    tmp = cm["tmp"]; idf = cm["idf"]
    xin = cm["xin"]; dmp = cm["dmp"]; h0t = cm["ha"]; h1t = cm["hb"]
    xi_tok = cm["otok"]
    if debug:
        dbg_h = ext["dbg_h"]; dbg_k = ext["dbg_k"]; dbg_ek = ext["dbg_ek"]; dbg_rw = ext["dbg_rw"]
    lb.push()
    aux = [lb.sb(f"aux{i}", [128, C]) for i in range(2)]
    sem_c = cm["sem_c"]
    mk_c = emit_vec_cols(lb, "mk_c", vecs["mixk"], sem_c)
    mv_c = emit_vec_cols(lb, "mv_c", vecs["mixv"], sem_c)
    mr_c = emit_vec_cols(lb, "mr_c", vecs["mixr"], sem_c)
    fk_c = emit_vec_cols(lb, "fk_c", vecs["fmixk"], sem_c)
    fr_c = emit_vec_cols(lb, "fr_c", vecs["fmixr"], sem_c)
    td_c = emit_vec_cols(lb, "td_c", vecs["tdec"], sem_c)
    tf_c = emit_vec_cols(lb, "tf_c", vecs["tfir"], sem_c)
    omk_c = lb.sb("omk_c", [128, NCT]); omv_c = lb.sb("omv_c", [128, NCT])
    omr_c = lb.sb("omr_c", [128, NCT]); ofk_c = lb.sb("ofk_c", [128, NCT])
    ofr_c = lb.sb("ofr_c", [128, NCT])
    eu_c = lb.sb("eu_c", [128, NCT]); ew_c = lb.sb("ew_c", [128, NCT])
    sem_d = lb.sem("deriv")
    for src, dst in ((mk_c, omk_c), (mv_c, omv_c), (mr_c, omr_c), (fk_c, ofk_c), (fr_c, ofr_c)):
        lb.op("vector", lambda e, s=src, d=dst: e.tensor_scalar(d[:], s[:], -1.0, 1.0, ALU.mult, ALU.add),
              waits=(sem_c.at(),), incs=(sem_d.inc(),))
    lb.op("scalar", lambda e: e.activation(eu_c[:], tf_c[:], AF.Exp), waits=(sem_c.at(),), incs=(sem_d.inc(),))
    lb.op("scalar", lambda e: e.activation(ew_c[:], td_c[:], AF.Exp), waits=(sem_c.at(),))
    lb.op("scalar", lambda e: e.activation(ew_c[:], ew_c[:], AF.Exp, scale=-1.0), incs=(sem_d.inc(),))
    zro = lb.sb("zro", [128, CHUNK])
    lb.op("vector", lambda e: e.memset(zro[:], 0.0), incs=(sem_d.inc(),))

    ld = lb.sem("ld"); ac = lb.sem("ac"); dv = lb.sem("dv")
    pe = lb.sem("pe"); ev = lb.sem("ev"); st = lb.sem("st")
    ldS = [lb.sem("ldS0"), lb.sem("ldS1")]
    ldA = [lb.sem("ldA0"), lb.sem("ldA1")]
    lb.op("sync", None, waits=(sem_c.at(),))

    lb.push()
    w1bc = emit_bcast_row(lb, "w1bc", vecs["ln1w"], sem_c)
    b1bc = emit_bcast_row(lb, "b1bc", vecs["ln1b"], sem_c)
    wk_s = lb.sb("wk_s", [128, NCT, C], BF16)
    wv_s = lb.sb("wv_s", [128, NCT, C], BF16)
    wr_s = lb.sb("wr_s", [128, NCT, C], BF16)
    wo_s = lb.sb("wo_s", [128, NCT, C], BF16)
    wsh_s = lb.sb("wsh_s", [128, NCT, C], BF16)
    sem_w = lb.sem("wts")
    for dram_ap, t in ((wkT, wk_s), (wvT, wv_s), (wrT, wr_s), (woT, wo_s), (wshT, wsh_s)):
        lb.op("gpsimd", lambda e, d=dram_ap, t=t: e.dma_start(
            out=t[:], in_=d.rearrange("(a p) o -> p a o", p=128)), incs=(sem_w.inc(16),))

    hTc = lb.sb("hTc", [128, NCT, CHUNK + 1], BF16)
    h0T = lb.sb("h0T", [128, NCT, CHUNK], BF16)
    xkT = lb.sb("xkT", [128, NCT, CHUNK], BF16)
    xvT = lb.sb("xvT", [128, NCT, CHUNK], BF16)
    xrT = lb.sb("xrT", [128, NCT, CHUNK], BF16)
    rwkvT = lb.sb("rwkvT", [128, NCT, CHUNK], BF16)
    xiT = lb.sb("xiT", [128, NCT, CHUNK])
    mixtmp = lb.sb("mixtmp", [128, CHUNK])
    ek_t = [lb.sb(f"ek_t{i}", [128, CHUNK]) for i in range(2)]
    vv_t = [lb.sb(f"vv_t{i}", [128, CHUNK]) for i in range(2)]
    sr_t = [lb.sb(f"sr_t{i}", [128, CHUNK]) for i in range(2)]
    ewb1 = [lb.sb(f"ewb1{i}", [128, CHUNK]) for i in range(2)]
    ekv_t = lb.sb("ekv_t", [128, CHUNK])
    num_t = dmp[:, 0:CHUNK]
    den_t = dmp[:, CHUNK:2 * CHUNK]
    _xif = xiT[:, 5:8, :].rearrange("p a t -> p (a t)")
    Sbuf = _xif[:, 0:CHUNK + 1]
    Dbuf = _xif[:, 768:768 + CHUNK + 1]
    Scar = lb.sb("Scar", [128, NCT]); Dcar = lb.sb("Dcar", [128, NCT])
    ps_b = [lb.ps(f"ps_b{i}", [128, 512]) for i in range(2)]
    ps_k = lb.ps("ps_k", [128, 512]); ps_v = lb.ps("ps_v", [128, 512])
    ps_r = lb.ps("ps_r", [128, 512]); ps_o = lb.ps("ps_o", [128, 512])
    ps_t = [lb.ps(f"ps_t{i}", [128, 512]) for i in range(2)]

    lb.op("vector", lambda e: e.memset(Scar[:], 0.0))
    lb.op("vector", lambda e: e.memset(Dcar[:], 0.0))
    lb.op("vector", lambda e: e.memset(hTc[:, :, 0:1], 0.0), incs=(sem_d.inc(),))

    mx = lb.sem("mx"); kvr = lb.sem("kvr"); evc = lb.sem("evc"); wkvd = lb.sem("wkvd")
    xio = lb.sem("xio"); xie = lb.sem("xie"); dbgs = lb.sem("dbgs"); psf = lb.sem("psf")
    lb._wk_hist = []

    for c in range(NCHUNK):
        if c > 0:
            lb.barrier()
            lb.op("vector", lambda e: e.tensor_copy(hTc[:, :, 0:1], hTc[:, :, CHUNK:CHUNK + 1]),
                  waits=(dbgs.at(),), incs=(dv.inc(),))
        for tj in range(TPC):
            ti = c * TPC + tj
            bi = tj % 2
            wfree = (dv.at(), ac.at()) if tj >= 2 else ()
            lb.op("sync", lambda e, ti=ti, bi=bi: e.dma_start(out=xin[bi][:], in_=x0[ti * 128:(ti + 1) * 128, :]),
                  waits=wfree, incs=(ldS[bi].inc(16),))
            emit_ln(lb, tmp, xin[bi], h0t, dmp, ac, dv,
                    wait_in=(ldS[bi].at(),) + ((pe.at(),) if (c > 0 or tj > 0) else ()))
            emit_transpose8(lb, h0t, lambda half, tj=tj: h0T[:, half * 4:(half + 1) * 4, tj * 128:(tj + 1) * 128],
                            ps_b, idf, pe, ev, (dv.at(), sem_c.at()))
            lb.op("scalar", lambda e: e.activation(tmp.q[:], tmp.r[:], AF.Square), incs=(ac.inc(),))
            lb.op("vector", lambda e: e.tensor_mul(tmp.v[:], tmp.s[:], tmp.q[:]), waits=(ac.at(),),
                  incs=(dv.inc(),))
            lb.op("scalar", lambda e: e.activation(tmp.v[:], tmp.v[:], AF.Ln, bias=1e-5), waits=(dv.at(),))
            lb.op("scalar", lambda e: e.activation(tmp.r[:], tmp.v[:], AF.Exp, scale=-0.5), incs=(ac.inc(),))
            lb.op("vector", lambda e: e.scalar_tensor_tensor(h1t[:], h0t[:], tmp.r[:], w1bc[:],
                                                             ALU.mult, ALU.mult), waits=(ac.at(),))
            lb.op("vector", lambda e: e.tensor_add(h1t[:], h1t[:], b1bc[:]), incs=(dv.inc(),))
            emit_transpose8(lb, h1t, lambda half, tj=tj: hTc[:, half * 4:(half + 1) * 4, 1 + tj * 128:1 + (tj + 1) * 128],
                            ps_b, idf, pe, ev, (dv.at(),))
        for ct in range(NCT):
            h_cur = hTc[:, ct, 1:1 + CHUNK]
            h_shf = hTc[:, ct, 0:CHUNK]
            for (mc, omc, dst) in ((mk_c, omk_c, xkT), (mv_c, omv_c, xvT), (mr_c, omr_c, xrT)):
                lb.op("vector", lambda e, h=h_cur, mc=mc, ct=ct: e.tensor_scalar_mul(
                    mixtmp[:], h, mc[:, ct:ct + 1]),
                    waits=(ev.at(), sem_d.at()) if ct == 0 else ())
                lb.op("vector", lambda e, hs=h_shf, omc=omc, dst=dst, ct=ct: e.scalar_tensor_tensor(
                    dst[:, ct, :], hs, omc[:, ct:ct + 1], mixtmp[:], ALU.mult, ALU.add),
                    incs=(mx.inc(),) if (ct == NCT - 1 and dst is xrT) else ())
        for o in range(NCT):
            for psd, ws, src in ((ps_k, wk_s, xkT), (ps_v, wv_s, xvT), (ps_r, wr_s, xrT)):
                for ct in range(NCT):
                    w = ()
                    if ct == 0 and psd is ps_k:
                        w = (mx.at(), sem_w.at(), psf.at())
                    lb.op("tensor", lambda e, o=o, ct=ct, psd=psd, ws=ws, src=src: e.matmul(
                        psd[:], ws[:, ct, o * 128:(o + 1) * 128], src[:, ct, :],
                        start=(ct == 0), stop=(ct == NCT - 1)),
                        waits=w, incs=(kvr.inc(),) if (ct == NCT - 1 and psd is ps_r) else ())
            po = o % 2
            wev = ((wkvd.h, lb._wk_hist[-2]),) if len(lb._wk_hist) >= 2 else ()
            lb.op("scalar", lambda e, po=po: e.activation(ek_t[po][:], ps_k[:], AF.Exp),
                  waits=(kvr.at(),) + wev)
            lb.op("scalar", lambda e, po=po: e.copy(vv_t[po][:], ps_v[:]))
            lb.op("scalar", lambda e, po=po: e.activation(sr_t[po][:], ps_r[:], AF.Exp, scale=-1.0),
                  incs=(psf.inc(),))
            lb.op("scalar", lambda e, po=po: e.activation(sr_t[po][:], sr_t[po][:], AF.Ln, bias=1.0))
            lb.op("scalar", lambda e, po=po: e.activation(sr_t[po][:], sr_t[po][:], AF.Exp, scale=-1.0))
            lb.op("scalar", lambda e, o=o, po=po: e.activation(ewb1[po][:], zro[:], AF.Identity,
                                                               bias=ew_c[:, o:o + 1]),
                  waits=(sem_d.at(),), incs=(evc.inc(),))
            lb.op("vector", lambda e, o=o: e.tensor_copy(Sbuf[:, 0:1], Scar[:, o:o + 1]), waits=(evc.at(),))
            lb.op("vector", lambda e, o=o: e.tensor_copy(Dbuf[:, 0:1], Dcar[:, o:o + 1]))
            lb.op("vector", lambda e, po=po: e.tensor_mul(ekv_t[:], ek_t[po][:], vv_t[po][:]))
            lb.op("vector", lambda e, o=o, po=po: e.tensor_tensor_scan(
                Sbuf[:, 1:], ewb1[po][:], ekv_t[:], Scar[:, o:o + 1], ALU.mult, ALU.add))
            lb.op("vector", lambda e, o=o, po=po: e.tensor_tensor_scan(
                Dbuf[:, 1:], ewb1[po][:], ek_t[po][:], Dcar[:, o:o + 1], ALU.mult, ALU.add))
            lb.op("vector", lambda e, o=o: e.tensor_copy(Scar[:, o:o + 1], Sbuf[:, CHUNK:CHUNK + 1]))
            lb.op("vector", lambda e, o=o: e.scalar_tensor_tensor(
                num_t[:], ekv_t[:], eu_c[:, o:o + 1], Sbuf[:, 0:CHUNK], ALU.mult, ALU.add))
            lb.op("vector", lambda e, o=o: e.tensor_copy(Dcar[:, o:o + 1], Dbuf[:, CHUNK:CHUNK + 1]))
            lb.op("vector", lambda e, o=o, po=po: e.scalar_tensor_tensor(
                den_t[:], ek_t[po][:], eu_c[:, o:o + 1], Dbuf[:, 0:CHUNK], ALU.mult, ALU.add))
            lb.op("vector", lambda e: e.reciprocal(den_t[:], den_t[:]))
            lb.op("vector", lambda e: e.tensor_mul(num_t[:], num_t[:], den_t[:]))
            lb.op("vector", lambda e, o=o, po=po: e.tensor_mul(rwkvT[:, o, :], num_t[:], sr_t[po][:]),
                  incs=(wkvd.inc(),))
            lb._wk_hist.append(wkvd.n)
        for o in range(NCT):
            for ct in range(NCT):
                w = (wkvd.at(), xie.at()) if ct == 0 else ()
                lb.op("tensor", lambda e, o=o, ct=ct: e.matmul(
                    ps_o[:], wo_s[:, ct, o * 128:(o + 1) * 128], rwkvT[:, ct, :],
                    start=(ct == 0), stop=False), waits=w)
            for ct in range(NCT):
                lb.op("tensor", lambda e, o=o, ct=ct: e.matmul(
                    ps_o[:], wsh_s[:, ct, o * 128:(o + 1) * 128], h0T[:, ct, :],
                    start=False, stop=(ct == NCT - 1)),
                    incs=(xio.inc(),) if ct == NCT - 1 else ())
            lb.op("vector", lambda e, o=o: e.tensor_copy(xiT[:, o, :], ps_o[:]),
                  waits=(xio.at(),), incs=(xie.inc(),))
        for tj in range(TPC):
            ti = c * TPC + tj
            for half in range(2):
                pbuf = ps_t[half]
                for q in range(4):
                    ct = half * 4 + q
                    w = (xie.at(), st.at(), ev.at()) if q == 0 else ()
                    lb.op("tensor", lambda e, tj=tj, ct=ct, q=q, pbuf=pbuf: e.transpose(
                        pbuf[:, q * 128:(q + 1) * 128], xiT[:, ct, tj * 128:(tj + 1) * 128], idf[:]),
                        waits=w, incs=(pe.inc(),) if q == 3 else ())
                lb.op("scalar", lambda e, half=half, pbuf=pbuf: e.copy(
                    xi_tok[:, half * 512:(half + 1) * 512], pbuf[:]),
                    waits=(pe.at(),), incs=(ev.inc(),))
            lb.op("sync", lambda e, ti=ti: e.dma_start(
                out=xi_scr[ti * 128:(ti + 1) * 128, :], in_=xi_tok[:]),
                waits=(ev.at(),), incs=(st.inc(16),))
        if debug and c == 0:
            lb.op("sync", lambda e: e.dma_start(out=dbg_h, in_=hTc[:]), waits=(ev.at(),), incs=(dbgs.inc(16),))
            lb.op("sync", lambda e: e.dma_start(out=dbg_k, in_=xkT[:]), waits=(mx.at(),), incs=(dbgs.inc(16),))
            lb.op("sync", lambda e: e.dma_start(out=dbg_rw, in_=rwkvT[:]), waits=(wkvd.at(),), incs=(dbgs.inc(16),))
            lb.op("sync", lambda e: e.dma_start(out=dbg_ek[:, 0, :], in_=Sbuf[:, 1:]), waits=(wkvd.at(),), incs=(dbgs.inc(16),))
            lb.op("sync", lambda e: e.dma_start(out=dbg_ek[:, 1, :], in_=Dbuf[:, 1:]), incs=(dbgs.inc(16),))
            lb.op("sync", lambda e: e.dma_start(out=dbg_ek[:, 2, :], in_=ewb1[:]), incs=(dbgs.inc(16),))
    lb.op("sync", lambda e: e.nop(), waits=(st.at(),))
    lb.barrier()
    lb.pop()

    lb.push()
    w2bc = emit_bcast_row(lb, "w2bc", vecs["ln2w"], sem_c)
    b2bc = emit_bcast_row(lb, "b2bc", vecs["ln2b"], sem_c)
    fwk_s = lb.sb("fwk_s", [128, NCT, 4 * C], BF16)
    fwr_s = lb.sb("fwr_s", [128, NCT, C], BF16)
    sem_fw = lb.sem("fw")
    lb.op("gpsimd", lambda e: e.dma_start(out=fwk_s[:], in_=fwkT.rearrange("(a p) o -> p a o", p=128)),
          incs=(sem_fw.inc(16),))
    lb.op("gpsimd", lambda e: e.dma_start(out=fwr_s[:], in_=fwrT.rearrange("(a p) o -> p a o", p=128)),
          incs=(sem_fw.inc(16),))
    gTc = lb.sb("gTc", [128, NCT, CHUNK + 1], BF16)
    gkT = lb.sb("gkT", [128, NCT, CHUNK], BF16)
    grT = lb.sb("grT", [128, NCT, CHUNK], BF16)
    mixtmp2 = lb.sb("mixtmp2", [128, CHUNK])
    rl_p = [lb.sb(f"rl_p{i}", [128, CHUNK]) for i in range(2)]
    sg_t = lb.sb("sg_t", [128, CHUNK])
    kkb = [lb.sb(f"kkb{i}", [128, CHUNK], BF16) for i in range(4)]
    ps2_b = [lb.ps(f"ps2_b{i}", [128, 512]) for i in range(2)]
    ps2_k = [lb.ps(f"ps2_k{i}", [128, 512]) for i in range(2)]
    ps2_r = lb.ps("ps2_r", [128, 512])
    kkS = [lb.sem(f"kkS{i}") for i in range(4)]
    mx2 = lb.sem("mx2"); kf = lb.sem("kf"); rlv = lb.sem("rlv")
    lb._rl_hist = []

    lb.op("vector", lambda e: e.memset(gTc[:, :, 0:1], 0.0), incs=(dv.inc(),))
    for c in range(NCHUNK):
        if c > 0:
            lb.barrier()
            lb.op("vector", lambda e: e.tensor_copy(gTc[:, :, 0:1], gTc[:, :, CHUNK:CHUNK + 1]),
                  incs=(dv.inc(),))
        for tj in range(TPC):
            ti = c * TPC + tj
            bi = tj % 2
            wfree = (dv.at(), ac.at()) if tj >= 2 else ()
            lb.op("sync", lambda e, ti=ti, bi=bi: e.dma_start(out=xin[bi][:], in_=xi_scr[ti * 128:(ti + 1) * 128, :]),
                  waits=wfree, incs=(ldS[bi].inc(16),))
            emit_ln(lb, tmp, xin[bi], h1t, dmp, ac, dv,
                    wait_in=(ldS[bi].at(),) + ((pe.at(),) if (c > 0 or tj > 0) else ()))
            lb.op("vector", lambda e: e.tensor_mul(h1t[:], h1t[:], w2bc[:]))
            lb.op("vector", lambda e: e.tensor_add(h1t[:], h1t[:], b2bc[:]), incs=(dv.inc(),))
            emit_transpose8(lb, h1t, lambda half, tj=tj: gTc[:, half * 4:(half + 1) * 4, 1 + tj * 128:1 + (tj + 1) * 128],
                            ps2_b, idf, pe, ev, (dv.at(),))
        for ct in range(NCT):
            h_cur = gTc[:, ct, 1:1 + CHUNK]
            h_shf = gTc[:, ct, 0:CHUNK]
            for (mc, omc, dst) in ((fk_c, ofk_c, gkT), (fr_c, ofr_c, grT)):
                lb.op("vector", lambda e, h=h_cur, mc=mc, ct=ct: e.tensor_scalar_mul(
                    mixtmp2[:], h, mc[:, ct:ct + 1]), waits=(ev.at(),) if ct == 0 else ())
                lb.op("vector", lambda e, hs=h_shf, omc=omc, dst=dst, ct=ct: e.scalar_tensor_tensor(
                    dst[:, ct, :], hs, omc[:, ct:ct + 1], mixtmp2[:], ALU.mult, ALU.add),
                    incs=(mx2.inc(),) if (ct == NCT - 1 and dst is grT) else ())
        for f in range(4 * C // 128):
            bf = f % 2
            psd = ps2_k[bf]
            for ct in range(NCT):
                w = (mx2.at(), sem_fw.at(), evc.at()) if ct == 0 else ()
                lb.op("tensor", lambda e, f=f, ct=ct, psd=psd: e.matmul(
                    psd[:], fwk_s[:, ct, f * 128:(f + 1) * 128], gkT[:, ct, :],
                    start=(ct == 0), stop=(ct == NCT - 1)),
                    waits=w, incs=(kf.inc(),) if ct == NCT - 1 else ())
            wrl2 = ((rlv.h, lb._rl_hist[-2]),) if len(lb._rl_hist) >= 2 else ()
            lb.op("scalar", lambda e, psd=psd, bf=bf: e.activation(rl_p[bf][:], psd[:], AF.Relu),
                  waits=(kf.at(),) + wrl2, incs=(evc.inc(),))
            sl = f % 4
            lb.op("vector", lambda e, bf=bf, sl=sl: e.tensor_mul(kkb[sl][:], rl_p[bf][:], rl_p[bf][:]),
                  waits=(evc.at(), kkS[sl].at()), incs=(rlv.inc(),))
            lb._rl_hist.append(rlv.n)
            lb.op("sync", lambda e, c=c, f=f, sl=sl: e.dma_start(
                out=kk_scr[f * 128:(f + 1) * 128, c * CHUNK:(c + 1) * CHUNK], in_=kkb[sl][:]),
                waits=(rlv.at(),), incs=(kkS[sl].inc(16),))
        for o in range(NCT):
            for ct in range(NCT):
                w = (mx2.at(), evc.at()) if ct == 0 else ()
                lb.op("tensor", lambda e, o=o, ct=ct: e.matmul(
                    ps2_r[:], fwr_s[:, ct, o * 128:(o + 1) * 128], grT[:, ct, :],
                    start=(ct == 0), stop=(ct == NCT - 1)),
                    waits=w, incs=(kf.inc(),) if ct == NCT - 1 else ())
            lb.op("scalar", lambda e, o=o: e.activation(sg_t[:], ps2_r[:], AF.Exp, scale=-1.0),
                  waits=(kf.at(), rlv.at()))
            lb.op("scalar", lambda e, o=o: e.activation(sg_t[:], sg_t[:], AF.Ln, bias=1.0))
            lb.op("scalar", lambda e, o=o: e.activation(sg_t[:], sg_t[:], AF.Exp, scale=-1.0),
                  incs=(evc.inc(),))
            sl = o % 4
            lb.op("vector", lambda e, sl=sl: e.tensor_copy(kkb[sl][:], sg_t[:]),
                  waits=(evc.at(), kkS[sl].at()), incs=(rlv.inc(),))
            lb.op("sync", lambda e, c=c, o=o, sl=sl: e.dma_start(
                out=srr_scr[o * 128:(o + 1) * 128, c * CHUNK:(c + 1) * CHUNK], in_=kkb[sl][:]),
                waits=(rlv.at(),), incs=(kkS[sl].inc(16),))
    lb.op("sync", lambda e: e.nop(), waits=tuple(s.at() for s in kkS))
    lb.barrier()
    lb.pop()

    lb.push()
    fwv_s = lb.sb("fwv_s", [128, 32, C], BF16)
    sem_fv = lb.sem("fv")
    lb.op("gpsimd", lambda e: e.dma_start(out=fwv_s[:], in_=fwvT.rearrange("(a p) o -> p a o", p=128)),
          incs=(sem_fv.inc(16),))
    kkc = lb.sb("kkc", [128, 32, CHUNK], BF16)
    srrc = lb.sb("srrc", [128, NCT, CHUNK], BF16)
    rkvT = lb.sb("rkvT", [128, NCT, CHUNK])
    ps3_o = lb.ps("ps3_o", [128, 512])
    ps3_t = [lb.ps(f"ps3_t{i}", [128, 512]) for i in range(2)]
    ldk = lb.sem("ldk"); kvo = lb.sem("kvo"); kve = lb.sem("kve")

    for c in range(NCHUNK):
        if c > 0:
            lb.barrier()
        lb.op("sync", lambda e, c=c: e.dma_start(
            out=kkc[:], in_=kk_scr.rearrange("(a p) t -> p a t", p=128)[:, :, c * CHUNK:(c + 1) * CHUNK]),
            incs=(ldk.inc(16),))
        lb.op("sync", lambda e, c=c: e.dma_start(
            out=srrc[:], in_=srr_scr.rearrange("(a p) t -> p a t", p=128)[:, :, c * CHUNK:(c + 1) * CHUNK]),
            incs=(ldk.inc(16),))
        for o in range(NCT):
            for f in range(32):
                w = (ldk.at(), sem_fv.at(), kve.at()) if f == 0 else ()
                lb.op("tensor", lambda e, o=o, f=f: e.matmul(
                    ps3_o[:], fwv_s[:, f, o * 128:(o + 1) * 128], kkc[:, f, :],
                    start=(f == 0), stop=(f == 31)),
                    waits=w, incs=(kvo.inc(),) if f == 31 else ())
            lb.op("vector", lambda e, o=o: e.tensor_mul(rkvT[:, o, :], srrc[:, o, :], ps3_o[:]),
                  waits=(kvo.at(),), incs=(kve.inc(),))
        for tj in range(TPC):
            ti = c * TPC + tj
            lb.op("sync", lambda e, ti=ti, tj=tj: e.dma_start(
                out=xin[tj % 2][:], in_=x0[ti * 128:(ti + 1) * 128, :]),
                waits=(dv.at(),) if tj >= 2 else (), incs=(ldS[tj % 2].inc(16),))
            lb.op("sync", lambda e, ti=ti, tj=tj: e.dma_start(
                out=aux[tj % 2][:], in_=xi_scr[ti * 128:(ti + 1) * 128, :]), incs=(ldA[tj % 2].inc(16),))
            for half in range(2):
                pbuf = ps3_t[half]
                for q in range(4):
                    ct = half * 4 + q
                    w = (kve.at(), st.at(), ev.at()) if q == 0 else ()
                    lb.op("tensor", lambda e, tj=tj, ct=ct, q=q, pbuf=pbuf: e.transpose(
                        pbuf[:, q * 128:(q + 1) * 128], rkvT[:, ct, tj * 128:(tj + 1) * 128], idf[:]),
                        waits=w, incs=(pe.inc(),) if q == 3 else ())
                lb.op("scalar", lambda e, half=half, pbuf=pbuf: e.copy(
                    xi_tok[:, half * 512:(half + 1) * 512], pbuf[:]),
                    waits=(pe.at(),), incs=(ev.inc(),))
            lb.op("vector", lambda e, tj=tj: e.tensor_add(xi_tok[:], aux[tj % 2][:], xi_tok[:]),
                  waits=(ev.at(), ldS[tj % 2].at(), ldA[tj % 2].at()))
            lb.op("vector", lambda e, tj=tj: e.tensor_add(xi_tok[:], xin[tj % 2][:], xi_tok[:]),
                  incs=(dv.inc(),))
            lb.op("sync", lambda e, ti=ti: e.dma_start(out=x1[ti * 128:(ti + 1) * 128, :], in_=xi_tok[:]),
                  waits=(dv.at(),), incs=(st.inc(16),))
    lb.op("sync", lambda e: e.nop(), waits=(st.at(),))
    lb.pop()
    lb.pop()


def decl_l0_attn(lb, debug=False, x_ap=None, out_ap=None):
    dbg_kind = "ExternalOutput" if debug else "Internal"
    ext = {}
    ext["x0"] = x_ap if x_ap is not None else lb.dram("x0", [T, C], F32, kind="ExternalInput").ap()
    for n, shp in (("wkT", [C, C]), ("wvT", [C, C]), ("wrT", [C, C]), ("woT", [C, C]),
                   ("wshT", [C, C]), ("fwkT", [C, 4 * C]), ("fwvT", [4 * C, C]), ("fwrT", [C, C])):
        ext[n] = lb.dram(n, shp, F32, kind="ExternalInput").ap()
    ext["vecs"] = {n: lb.dram(n, [C], F32, kind="ExternalInput").ap() for n in
                   ("ln1w", "ln1b", "ln2w", "ln2b", "mixk", "mixv", "mixr", "fmixk", "fmixr", "tdec", "tfir")}
    ext["x1"] = out_ap if out_ap is not None else lb.dram("x1", [T, C], F32, kind="ExternalOutput").ap()
    ext["xi_scr"] = lb.dram("xi_scr", [T, C], F32, kind=dbg_kind).ap()
    ext["kk_scr"] = lb.dram("kk_scr", [4 * C, T], BF16, kind=dbg_kind).ap()
    ext["srr_scr"] = lb.dram("srr_scr", [C, T], BF16, kind=dbg_kind).ap()
    if debug:
        ext["dbg_h"] = lb.dram("dbg_h", [128, NCT, CHUNK + 1], BF16, kind="ExternalOutput").ap()
        ext["dbg_k"] = lb.dram("dbg_k", [128, NCT, CHUNK], BF16, kind="ExternalOutput").ap()
        ext["dbg_ek"] = lb.dram("dbg_ek", [128, 3, CHUNK], F32, kind="ExternalOutput").ap()
        ext["dbg_rw"] = lb.dram("dbg_rw", [128, NCT, CHUNK], BF16, kind="ExternalOutput").ap()
    return ext


def build_l0_attn(debug=False):
    lb = LB()
    lb.ident_ap = lb.dram("i128f", [128, 128], F32, kind="ExternalInput").ap()
    lb.init_consts([1e-5, -0.5, -1.0, 1.0, 1.0 / 1024.0, LN8])
    ext = decl_l0_attn(lb, debug=debug)
    cm = make_common(lb)
    emit_l0_attn(lb, cm, ext, debug=debug)
    return lb.build()


def host_prep_l0attn(inp, x_core):
    ct = np.ascontiguousarray
    return {
        "x0": x_core,
        "wkT": ct(inp["L0_attn_att_Wk"].T), "wvT": ct(inp["L0_attn_att_Wv"].T),
        "wrT": ct(inp["L0_attn_att_Wr"].T), "woT": ct(inp["L0_attn_att_Wo"].T),
        "wshT": ct(inp["L0_attn_short_W"].T),
        "fwkT": ct(inp["L0_attn_ffn_Wk"].T), "fwvT": ct(inp["L0_attn_ffn_Wv"].T),
        "fwrT": ct(inp["L0_attn_ffn_Wr"].T),
        "ln1w": inp["L0_attn_ln1_w"], "ln1b": inp["L0_attn_ln1_b"],
        "ln2w": inp["L0_attn_ln2_w"], "ln2b": inp["L0_attn_ln2_b"],
        "mixk": inp["L0_attn_att_mix_k"], "mixv": inp["L0_attn_att_mix_v"],
        "mixr": inp["L0_attn_att_mix_r"],
        "fmixk": inp["L0_attn_ffn_mix_k"], "fmixr": inp["L0_attn_ffn_mix_r"],
        "tdec": inp["L0_attn_att_time_decay"], "tfir": inp["L0_attn_att_time_first"],
        "i128f": np.eye(128, dtype=np.float32),
    }


LN8 = 2.0794415416798357

def emit_mxq(lb, t_in, out_bf, bufs, ac, dv, wait_in=()):
    ab = bufs["ab"]; wk = bufs["wk"]; sc = bufs["sc"]; rs8 = bufs["rs8"]; s8 = bufs["s8"]
    y8 = bufs["y8"]; r8 = bufs["r8"]; y4 = bufs["y4"]; r4 = bufs["r4"]
    m1 = bufs["m1"]; m2 = bufs["m2"]
    v3 = lambda ap, g: ap.rearrange("p (b g) -> p b g", g=g)
    lb.op("scalar", lambda e: e.activation(ab[:], t_in[:], AF.Abs), waits=wait_in + (dv.at(),),
          incs=(ac.inc(),))
    lb.op("vector", lambda e: e.reduce_max(wk[:, 0:32], v3(ab[:], 32), axis=AX.X), waits=(ac.at(),))
    lb.op("vector", lambda e: e.tensor_scalar_mul(wk[:, 64:576], ab[:, 0:512], 1.0))
    lb.op("vector", lambda e: e.tensor_scalar(sc[:], wk[:, 0:32], 1.0 / 7.5, 1e-30,
                                              ALU.mult, ALU.max))
    lb.op("vector", lambda e: e.tensor_scalar_mul(wk[:, 64:576], ab[:, 0:512], 1.0), incs=(dv.inc(),))
    lb.op("scalar", lambda e: e.activation(rs8[:], sc[:], AF.Ln), waits=(dv.at(),))
    lb.op("scalar", lambda e: e.activation(rs8[:], rs8[:], AF.Exp, scale=-1.0, bias=LN8))
    lb.op("scalar", lambda e: e.activation(s8[:], sc[:], AF.Identity, scale=0.125), incs=(ac.inc(),))
    lb.op("vector", lambda e: e.tensor_tensor(v3(y8[:], 32), v3(t_in[:], 32),
                                              rs8[:].unsqueeze(2).broadcast_to([128, 32, 32]), ALU.mult),
          waits=(ac.at(),))
    lb.op("vector", lambda e: e.tensor_scalar(r8[:], y8[:], MAGIC, MAGIC, ALU.add, ALU.subtract))
    lb.op("vector", lambda e: e.tensor_scalar_mul(y4[:], y8[:], 0.5))
    lb.op("vector", lambda e: e.tensor_scalar(r4[:], y4[:], MAGIC, MAGIC, ALU.add, ALU.subtract))
    lb.op("vector", lambda e: e.tensor_scalar_mul(y4[:], y8[:], 0.25))
    lb.op("vector", lambda e: e.tensor_scalar(ab[:], y4[:], MAGIC, MAGIC, ALU.add, ALU.subtract),
          incs=(dv.inc(),))
    lb.op("scalar", lambda e: e.activation(y4[:], y8[:], AF.Abs), waits=(dv.at(),),
          incs=(ac.inc(),))
    lb.op("vector", lambda e: e.tensor_single_scalar(m1[:], y4[:], 16.0, ALU.is_lt), waits=(ac.at(),))
    lb.op("vector", lambda e: e.tensor_single_scalar(m2[:], y4[:], 32.0, ALU.is_lt))
    lb.op("vector", lambda e: e.tensor_scalar_mul(r4[:], r4[:], 2.0))
    lb.op("vector", lambda e: e.tensor_scalar_mul(ab[:], ab[:], 4.0))
    lb.op("vector", lambda e: e.select(y4[:], m2[:], r4[:], ab[:]))
    lb.op("vector", lambda e: e.select(y8[:], m1[:], r8[:], y4[:]))
    lb.op("vector", lambda e: e.tensor_tensor(v3(out_bf[:], 32), v3(y8[:], 32),
                                              s8[:].unsqueeze(2).broadcast_to([128, 32, 32]), ALU.mult),
          incs=(dv.inc(),))



def emit_l0_ffn(lb, cm, ext):
    lb.pfx = "B_"
    x1 = ext["x1"]; wgt = ext["wgt"]; bias = ext["bias"]; x2 = ext["x2"]
    tmp = cm["tmp"]; idf = cm["idf"]
    xin = cm["xin"]; dmp = cm["dmp"]; h_t = cm["ha"]; qd = cm["hb"]
    out_tok = cm["otok"]
    lb.push()
    sem_c = cm["sem_c"]
    bias_c = emit_vec_cols(lb, "bias_c", bias, sem_c)

    qb = {k: lb.sb("qb_" + k, [128, 1024]) for k in ("ab", "wk", "y8", "r8", "y4", "r4")}
    qb["sc"] = lb.sb("qb_sc", [128, 32]); qb["rs8"] = lb.sb("qb_rs8", [128, 32])
    qb["s8"] = lb.sb("qb_s8", [128, 32])
    qb["m1"] = lb.sb("qb_m1", [128, 1024], U8); qb["m2"] = lb.sb("qb_m2", [128, 1024], U8)
    wdT = lb.sb("wdT", [128, NCT, C], BF16)
    xdT = lb.sb("xdT", [128, NCT, CHUNK], BF16)
    outT = lb.sb("outT", [128, NCT, CHUNK])
    ps_b = [lb.ps(f"ps_b{i}", [128, 512]) for i in range(2)]
    ps_o = lb.ps("ps_o", [128, 512])
    ps_t = [lb.ps(f"ps_t{i}", [128, 512]) for i in range(2)]
    ld = lb.sem("ld"); ac = lb.sem("ac"); dv = lb.sem("dv")
    pe = lb.sem("pe"); ev = lb.sem("ev"); st = lb.sem("st")
    mmo = lb.sem("mmo"); mme = lb.sem("mme"); trd = lb.sem("trd")
    ldS = [lb.sem("ldS0"), lb.sem("ldS1")]

    for j in range(NCT):
        bi = j % 2
        wfree = (dv.at(), ac.at()) if j >= 2 else ()
        lb.op("sync", lambda e, j=j, bi=bi: e.dma_start(out=xin[bi][:], in_=wgt[j * 128:(j + 1) * 128, :]),
              waits=wfree, incs=(ldS[bi].inc(16),))
        emit_mxq(lb, xin[bi], qd, qb, ac, dv, wait_in=(ldS[bi].at(),) + ((pe.at(),) if j > 0 else ()))
        for half in range(2):
            pbuf = ps_b[half]
            for q in range(4):
                ct = half * 4 + q
                w = (dv.at(), sem_c.at(), ev.at()) if q == 0 else ()
                lb.op("tensor", lambda e, ct=ct, q=q, pbuf=pbuf: e.transpose(
                    pbuf[:, q * 128:(q + 1) * 128], qd[:, ct * 128:(ct + 1) * 128], idf[:]),
                    waits=w, incs=(pe.inc(),) if q == 3 else ())
            lb.op("scalar", lambda e, j=j, half=half, pbuf=pbuf: e.copy(
                wdT[:, half * 4:(half + 1) * 4, j * 128:(j + 1) * 128], pbuf[:]),
                waits=(pe.at(),), incs=(ev.inc(),))
    lb.barrier()

    for c in range(NCHUNK):
        if c > 0:
            lb.barrier()
        for tj in range(TPC):
            ti = c * TPC + tj
            bi = tj % 2
            wfree = (dv.at(), ac.at()) if tj >= 2 else ()
            lb.op("sync", lambda e, ti=ti, bi=bi: e.dma_start(out=xin[bi][:], in_=x1[ti * 128:(ti + 1) * 128, :]),
                  waits=wfree, incs=(ldS[bi].inc(16),))
            emit_ln(lb, tmp, xin[bi], h_t, dmp, ac, dv,
                    wait_in=(ldS[bi].at(),) + ((pe.at(),) if (c > 0 or tj > 0) else ()))
            emit_mxq(lb, h_t, qd, qb, ac, dv)
            for half in range(2):
                pbuf = ps_b[half]
                for q in range(4):
                    ct = half * 4 + q
                    w = (dv.at(), ev.at()) if q == 0 else ()
                    lb.op("tensor", lambda e, tj=tj, ct=ct, q=q, pbuf=pbuf: e.transpose(
                        pbuf[:, q * 128:(q + 1) * 128], qd[:, ct * 128:(ct + 1) * 128], idf[:]),
                        waits=w, incs=(pe.inc(),) if q == 3 else ())
                lb.op("scalar", lambda e, tj=tj, half=half, pbuf=pbuf: e.copy(
                    xdT[:, half * 4:(half + 1) * 4, tj * 128:(tj + 1) * 128], pbuf[:]),
                    waits=(pe.at(),), incs=(ev.inc(),))
        for o in range(NCT):
            for ct in range(NCT):
                w = (ev.at(), mme.at()) if ct == 0 else ()
                lb.op("tensor", lambda e, o=o, ct=ct: e.matmul(
                    ps_o[:], wdT[:, ct, o * 128:(o + 1) * 128], xdT[:, ct, :],
                    start=(ct == 0), stop=(ct == NCT - 1)),
                    waits=w, incs=(mmo.inc(),) if ct == NCT - 1 else ())
            lb.op("scalar", lambda e, o=o: e.activation(outT[:, o, :], ps_o[:], AF.Identity,
                                                        bias=bias_c[:, o:o + 1]),
                  waits=(mmo.at(),), incs=(mme.inc(),))
        for tj in range(TPC):
            ti = c * TPC + tj
            bi = tj % 2
            lb.op("sync", lambda e, ti=ti, bi=bi: e.dma_start(out=xin[bi][:], in_=x1[ti * 128:(ti + 1) * 128, :]),
                  waits=(st.at(),) if ti >= 2 else (), incs=(ldS[bi].inc(16),))
            for half in range(2):
                pbuf = ps_t[half]
                for q in range(4):
                    ct = half * 4 + q
                    w = (mme.at(), st.at(), trd.at()) if q == 0 else ()
                    lb.op("tensor", lambda e, tj=tj, ct=ct, q=q, pbuf=pbuf: e.transpose(
                        pbuf[:, q * 128:(q + 1) * 128], outT[:, ct, tj * 128:(tj + 1) * 128], idf[:]),
                        waits=w, incs=(pe.inc(),) if q == 3 else ())
                lb.op("scalar", lambda e, half=half, pbuf=pbuf: e.copy(
                    out_tok[:, half * 512:(half + 1) * 512], pbuf[:]),
                    waits=(pe.at(),), incs=(trd.inc(),))
            lb.op("vector", lambda e, bi=bi: e.tensor_add(out_tok[:], xin[bi][:], out_tok[:]),
                  waits=(trd.at(), ldS[bi].at()), incs=(dv.inc(),))
            lb.op("sync", lambda e, ti=ti: e.dma_start(out=x2[ti * 128:(ti + 1) * 128, :], in_=out_tok[:]),
                  waits=(dv.at(),), incs=(st.inc(16),))
    lb.op("sync", lambda e: e.nop(), waits=(st.at(),))
    lb.pop()


def decl_l0_ffn(lb, x_ap=None, out_ap=None):
    ext = {}
    ext["x1"] = x_ap if x_ap is not None else lb.dram("x1", [T, C], F32, kind="ExternalInput").ap()
    ext["wgt"] = lb.dram("wgt", [C, C], F32, kind="ExternalInput").ap()
    ext["bias"] = lb.dram("bias", [C], F32, kind="ExternalInput").ap()
    ext["x2"] = out_ap if out_ap is not None else lb.dram("x2", [T, C], F32, kind="ExternalOutput").ap()
    return ext


def build_l0_ffn():
    lb = LB()
    lb.ident_ap = lb.dram("i128f", [128, 128], F32, kind="ExternalInput").ap()
    lb.init_consts([1e-5, -0.5, -1.0, 1.0, 1.0 / 1024.0, LN8])
    ext = decl_l0_ffn(lb)
    cm = make_common(lb)
    emit_l0_ffn(lb, cm, ext)
    return lb.build()


def host_prep_l0ffn(inp, x1_core):
    return {
        "x1": x1_core,
        "wgt": inp["L0_ffn_weight"],
        "bias": inp["L0_ffn_bias"],
        "i128f": np.eye(128, dtype=np.float32),
    }


DI = 2048
NDT = DI // 128
NST = 16

def emit_l1_attn(lb, cm, ext):
    lb.pfx = "C_"
    x2 = ext["x2"]; win = ext["win"]; bin_ = ext["bin"]; wx = ext["wx"]; bx = ext["bx"]
    wdt = ext["wdt"]; bdt = ext["bdt"]; alog = ext["alog"]; dp_v = ext["dp_v"]
    wout = ext["wout"]; bout = ext["bout"]; sel16 = ext["sel16"]; x3 = ext["x3"]
    tmp = cm["tmp"]; idf = cm["idf"]
    xin = cm["xin"]; dmp = cm["dmp"]; h_t = cm["ha"]; out_tok = cm["otok"]
    lb.push()
    sem_c = cm["sem_c"]
    binc = emit_vec_cols(lb, "binc", bin_, sem_c, n=2 * DI)
    bdtc = emit_vec_cols(lb, "bdtc", bdt, sem_c, n=DI)
    dpc = emit_vec_cols(lb, "dpc", dp_v, sem_c, n=DI)
    boutc = emit_vec_cols(lb, "boutc", bout, sem_c, n=C)
    bx_s = lb.sb("bx_s", [128, 3])
    lb.op("sync", lambda e: e.dma_start(out=bx_s[0:64, 0:1], in_=bx[0:64].rearrange("(p one) -> p one", one=1)),
          incs=(sem_c.inc(16),))
    lb.op("sync", lambda e: e.dma_start(out=bx_s[0:16, 1:2], in_=bx[64:80].rearrange("(p one) -> p one", one=1)),
          incs=(sem_c.inc(16),))
    lb.op("sync", lambda e: e.dma_start(out=bx_s[0:16, 2:3], in_=bx[80:96].rearrange("(p one) -> p one", one=1)),
          incs=(sem_c.inc(16),))
    sel_s = lb.sb("sel_s", [16, 16 * 128], BF16)
    lb.op("sync", lambda e: e.dma_start(out=sel_s[:], in_=sel16), incs=(sem_c.inc(16),))
    a_s = lb.sb("a_s", [128, NDT, NST])
    lb.op("sync", lambda e: e.dma_start(out=a_s[:], in_=alog.rearrange("(a p) n -> p a n", p=128)),
          incs=(sem_c.inc(16),))
    lb.op("sync", None, waits=(sem_c.at(),))
    sem_a = lb.sem("a")
    lb.op("scalar", lambda e: e.activation(a_s[:], a_s[:], AF.Exp), waits=(sem_c.at(),), incs=(sem_a.inc(),))
    lb.op("vector", lambda e: e.tensor_scalar_mul(a_s[:], a_s[:], -1.0), waits=(sem_a.at(),), incs=(sem_a.inc(),))

    wx_s = lb.sb("wx_s", [128, NDT, 96], BF16)
    wdt_s = lb.sb("wdt_s", [128, DI], BF16)
    sem_w = lb.sem("wts")
    lb.op("gpsimd", lambda e: e.dma_start(out=wx_s[:], in_=wx.rearrange("(a p) o -> p a o", p=128)),
          incs=(sem_w.inc(16),))
    lb.op("gpsimd", lambda e: e.dma_start(out=wdt_s[0:64, :], in_=wdt), incs=(sem_w.inc(16),))
    win_s = lb.sb("win_s", [128, NCT, DI], BF16)
    wo_s = [lb.sb(f"wo_s{i}", [128, NDT, 128], BF16) for i in range(2)]

    h2T = lb.sb("h2T", [128, NCT, CHUNK], BF16)
    xsT = lb.sb("xsT", [128, NDT, CHUNK], BF16)
    dp_s = lb.sb("dp_s", [128, CHUNK], BF16)
    bs_s = lb.sb("bs_s", [128, CHUNK], BF16)
    cs_s = lb.sb("cs_s", [128, CHUNK], BF16)
    bbc = lb.sb("bbc", [128, NST, CHUNK], BF16)
    cbc = lb.sb("cbc", [128, NST, CHUNK], BF16)
    hcb = win_s[:, 0:4, :].rearrange("p a (b t) -> p (a b) t", t=CHUNK)
    y2T = lb.sb("y2T", [128, NDT, CHUNK], BF16)
    outT = lb.sb("outT", [128, NCT, CHUNK])
    st_s = lb.sb("st_s", [128, NDT, NST])
    zt_t = lb.sb("zt_t", [128, CHUNK]); az_t = lb.sb("az_t", [128, CHUNK])
    delta_t = [lb.sb(f"delta_t{i}", [128, CHUNK]) for i in range(2)]
    dx_bf = [lb.sb(f"dx_bf{i}", [128, CHUNK], BF16) for i in range(2)]
    u_bf = lb.sb("u_bf", [128, CHUNK], BF16)
    hb_r = [lb.sb(f"hb_r{i}", [128, CHUNK], BF16) for i in range(4)]
    da_t = [lb.sb(f"da_t{i}", [128, CHUNK], BF16) for i in range(4)]
    szT = lb.sb("szT", [128, NDT, CHUNK], BF16)
    rl_t = lb.sb("rl_t", [128, CHUNK])
    y_t = dmp[:, 0:CHUNK]
    t1_t = dmp[:, CHUNK:2 * CHUNK]

    ps_b = [lb.ps(f"ps_b{i}", [128, 512]) for i in range(2)]
    ps_a = lb.ps("ps_a", [128, 512])
    ps_c = lb.ps("ps_c", [128, 512])
    ps_d = lb.ps("ps_d", [128, 512])
    ps_e = lb.ps("ps_e", [128, 512])
    ps_fp = [lb.ps(f"ps_fp{i}", [128, 512]) for i in range(2)]
    ps_f = ps_fp[0]
    ps_o = ps_fp[1]

    ld = lb.sem("ld"); ac = lb.sem("ac"); dv = lb.sem("dv")
    pe = lb.sem("pe"); ev = lb.sem("ev"); st = lb.sem("st")
    mmo = lb.sem("mmo"); mme = lb.sem("mme"); ldw = lb.sem("ldw")
    scn = lb.sem("scn"); psn = lb.sem("psn"); bce = lb.sem("bce"); wos = lb.sem("wos")
    ldS = [lb.sem("ldS0"), lb.sem("ldS1")]
    ldwO = [lb.sem("ldwO0"), lb.sem("ldwO1")]
    lb._scn_hist = []; lb._scan_hist = []; lb._hc_hist = []
    lb._sp_hist = []; lb._da_hist = []; lb._dx_hist = []

    lb.op("vector", lambda e: e.memset(st_s[:], 0.0), incs=(dv.inc(),))

    for c in range(NCHUNK):
        if c > 0:
            lb.barrier()
        lb.op("gpsimd", lambda e: e.dma_start(
            out=win_s[:], in_=win.rearrange("(a p) o -> p a o", p=128)[:, :, 0:DI]),
            incs=(ldw.inc(16),))
        for tj in range(TPC):
            ti = c * TPC + tj
            bi = tj % 2
            wfree = (dv.at(), ac.at()) if tj >= 2 else ()
            lb.op("sync", lambda e, ti=ti, bi=bi: e.dma_start(out=xin[bi][:], in_=x2[ti * 128:(ti + 1) * 128, :]),
                  waits=wfree, incs=(ldS[bi].inc(16),))
            emit_ln(lb, tmp, xin[bi], h_t, dmp, ac, dv,
                    wait_in=(ldS[bi].at(),) + ((pe.at(),) if (c > 0 or tj > 0) else ()))
            emit_transpose8(lb, h_t, lambda half, tj=tj: h2T[:, half * 4:(half + 1) * 4, tj * 128:(tj + 1) * 128],
                            ps_b, idf, pe, ev, (dv.at(), sem_c.at()))
        mme_xs = []
        for dt in range(NDT):
            pxs = ps_a if dt % 2 == 0 else ps_fp[0]
            if dt == 0:
                w = (ev.at(), ldw.at(), mme.at())
            elif dt >= 2:
                w = ((mme.h, mme_xs[dt - 2]),)
            else:
                w = ()
            for ct in range(NCT):
                lb.op("tensor", lambda e, dt=dt, ct=ct, pxs=pxs: e.matmul(
                    pxs[:], win_s[:, ct, dt * 128:(dt + 1) * 128], h2T[:, ct, :],
                    start=(ct == 0), stop=(ct == NCT - 1)),
                    waits=w if ct == 0 else (), incs=(mmo.inc(),) if ct == NCT - 1 else ())
            lb.op("scalar", lambda e, dt=dt, pxs=pxs: e.activation(xsT[:, dt, :], pxs[:], AF.Silu,
                                                          bias=binc[:, dt:dt + 1]),
                  waits=(mmo.at(),), incs=(mme.inc(),))
            mme_xs.append(mme.n)
        for (psd, colr, outb, bxcol) in ((ps_c, (0, 64), dp_s, 0), (ps_d, (64, 80), bs_s, 1),
                                         (ps_e, (80, 96), cs_s, 2)):
            np_ = colr[1] - colr[0]
            for dt in range(NDT):
                w = (mme.at(), sem_w.at(), bce.at()) if dt == 0 else ()
                lb.op("tensor", lambda e, dt=dt, psd=psd, colr=colr, np_=np_: e.matmul(
                    psd[0:np_, :], wx_s[:, dt, colr[0]:colr[1]], xsT[:, dt, :],
                    start=(dt == 0), stop=(dt == NDT - 1)),
                    waits=w, incs=(mmo.inc(),) if dt == NDT - 1 else ())
            lb.op("scalar", lambda e, psd=psd, outb=outb, np_=np_, bxcol=bxcol: e.activation(
                outb[0:np_, :], psd[0:np_, :], AF.Identity, bias=bx_s[0:np_, bxcol:bxcol + 1]),
                waits=(mmo.at(),), incs=(mme.inc(),))
        for n in range(NST):
            w = (mme.at(), bce.at()) if n == 0 else (mme.at(),)
            lb.op("tensor", lambda e, n=n: e.matmul(
                ps_f[:], sel_s[:, n * 128:(n + 1) * 128], bs_s[0:16, :], start=True, stop=True),
                waits=w, incs=(mmo.inc(),))
            lb.op("scalar", lambda e, n=n: e.copy(bbc[:, n, :], ps_f[:]), waits=(mmo.at(),), incs=(mme.inc(),))
            lb.op("tensor", lambda e, n=n: e.matmul(
                ps_f[:], sel_s[:, n * 128:(n + 1) * 128], cs_s[0:16, :], start=True, stop=True),
                waits=(mme.at(),), incs=(mmo.inc(),))
            lb.op("scalar", lambda e, n=n: e.copy(cbc[:, n, :], ps_f[:]), waits=(mmo.at(),), incs=(mme.inc(),))
        lb.op("scalar", lambda e: e.engine_nop() if False else e.copy(az_t[:, 0:1], bbc[:, 0, 0:1]),
              incs=(bce.inc(),))
        lb.op("gpsimd", lambda e: e.dma_start(
            out=win_s[:], in_=win.rearrange("(a p) o -> p a o", p=128)[:, :, DI:2 * DI]),
            waits=(mme.at(),), incs=(ldw.inc(16),))
        for dt in range(NDT):
            for ct in range(NCT):
                w = (ldw.at(), mme.at()) if ct == 0 else ()
                lb.op("tensor", lambda e, dt=dt, ct=ct: e.matmul(
                    ps_a[:], win_s[:, ct, dt * 128:(dt + 1) * 128], h2T[:, ct, :],
                    start=(ct == 0), stop=(ct == NCT - 1)),
                    waits=w, incs=(mmo.inc(),) if ct == NCT - 1 else ())
            lb.op("scalar", lambda e, dt=dt: e.activation(szT[:, dt, :], ps_a[:], AF.Silu,
                                                          bias=binc[:, 16 + dt:17 + dt]),
                  waits=(mmo.at(),), incs=(mme.inc(),))
        for dt in range(NDT):
            par = dt % 2
            if dt == 0:
                w5a = (mme.at(), lb._scn_hist[-1] if False else scn.at())
            elif dt >= 2:
                w5a = ((ac.h, lb._sp_hist[-2]),)
            else:
                w5a = ()
            lb.op("tensor", lambda e, dt=dt, par=par: e.matmul(
                ps_fp[par][:], wdt_s[0:64, dt * 128:(dt + 1) * 128], dp_s[0:64, :], start=True, stop=True),
                waits=w5a, incs=(mmo.inc(),))
            wzt = (mmo.at(),) + (((dv.h, lb._dx_hist[-1]),) if len(lb._dx_hist) >= 1 else ())
            lb.op("scalar", lambda e, dt=dt, par=par: e.activation(zt_t[:], ps_fp[par][:], AF.Identity,
                                                                   bias=bdtc[:, dt:dt + 1]), waits=wzt)
            lb.op("scalar", lambda e: e.activation(az_t[:], zt_t[:], AF.Abs))
            lb.op("scalar", lambda e: e.activation(az_t[:], az_t[:], AF.Exp, scale=-1.0))
            lb.op("scalar", lambda e: e.activation(az_t[:], az_t[:], AF.Ln, bias=1.0), incs=(ac.inc(),))
            lb._sp_hist.append(ac.n)
            wrl = (ac.at(),) + (((ac.h, lb._da_hist[-2]),) if len(lb._da_hist) >= 2 else ())
            lb.op("vector", lambda e: e.tensor_single_scalar(rl_t[:], zt_t[:], 0.0, ALU.max),
                  waits=wrl)
            lb.op("vector", lambda e, par=par: e.tensor_add(delta_t[par][:], az_t[:], rl_t[:]))
            lb.op("vector", lambda e, dt=dt, par=par: e.tensor_mul(dx_bf[par][:], delta_t[par][:], xsT[:, dt, :]),
                  incs=(dv.inc(),))
            lb._dx_hist.append(dv.n)
            for n in range(NST):
                bi = n % 4
                if len(lb._scan_hist) >= 4:
                    wda = ((scn.h, lb._scan_hist[-4]),)
                else:
                    wda = ()
                wdelta = ((dv.h, lb._dx_hist[-1]),) if n == 0 else ()
                if dt == 0 and n == 0:
                    wdelta = wdelta + (sem_a.at(),)
                lb.op("scalar", lambda e, dt=dt, n=n, bi=bi, par=par: e.activation(
                    da_t[bi][:], delta_t[par][:], AF.Exp, scale=a_s[:, dt, n:n + 1]),
                    waits=wda + wdelta, incs=(ac.inc(),))
                if n == 15:
                    lb._da_hist.append(ac.n)
                hb = hb_r[n % 4]
                lb.op("vector", lambda e, n=n, par=par: e.tensor_mul(u_bf[:], dx_bf[par][:], bbc[:, n, :]))
                lb.op("vector", lambda e, dt=dt, n=n, bi=bi, hb=hb: e.tensor_tensor_scan(
                    hb[:], da_t[bi][:], u_bf[:], st_s[:, dt, n:n + 1], ALU.mult, ALU.add),
                    waits=(ac.at(),), incs=(scn.inc(),))
                lb._scan_hist.append(scn.n)
                lb.op("vector", lambda e, n=n, hb=hb: e.tensor_mul(hcb[:, n, :], hb[:], cbc[:, n, :]),
                      incs=(psn.inc(),))
                lb._hc_hist.append(psn.n)
                if n >= 1:
                    lb.op("vector", lambda e, dt=dt, n=n: e.tensor_copy(
                        st_s[:, dt, n - 1:n], hb_r[(n - 1) % 4][:, CHUNK - 1:CHUNK]))
            lb.op("vector", lambda e, dt=dt: e.tensor_copy(st_s[:, dt, 15:16], hb_r[15 % 4][:, CHUNK - 1:CHUNK]))
            red = outT[:]
            lb.op("vector", lambda e, red=red: e.tensor_tensor(
                red[:, 0:8, :], hcb[:, 0:8, :], hcb[:, 8:16, :], ALU.add))
            lb.op("vector", lambda e, red=red: e.tensor_tensor(
                red[:, 0:4, :], red[:, 0:4, :], red[:, 4:8, :], ALU.add))
            lb.op("vector", lambda e, red=red: e.tensor_tensor(
                red[:, 0:2, :], red[:, 0:2, :], red[:, 2:4, :], ALU.add))
            lb.op("vector", lambda e, red=red: e.tensor_tensor(
                y_t[:], red[:, 0, :], red[:, 1, :], ALU.add))
            lb.op("vector", lambda e, dt=dt: e.scalar_tensor_tensor(
                t1_t[:], xsT[:, dt, :], dpc[:, dt:dt + 1], y_t[:], ALU.mult, ALU.add))
            lb.op("vector", lambda e, dt=dt: e.tensor_mul(y2T[:, dt, :], t1_t[:], szT[:, dt, :]),
                  incs=(dv.inc(),))
        mme_after_evac = []
        for o in range(NCT):
            bi = o % 2
            w_buf = ((mme.h, mme_after_evac[o - 2]),) if o >= 2 else ()
            lb.op("gpsimd", lambda e, o=o, bi=bi: e.dma_start(
                out=wo_s[bi][:], in_=wout.rearrange("(a p) o -> p a o", p=128)[:, :, o * 128:(o + 1) * 128]),
                waits=w_buf, incs=(ldwO[bi].inc(16),))
            for dt in range(NDT):
                w = (dv.at(), ldwO[bi].at(), mme.at()) if dt == 0 else ()
                lb.op("tensor", lambda e, o=o, dt=dt, bi=bi: e.matmul(
                    ps_c[:], wo_s[bi][:, dt, :], y2T[:, dt, :],
                    start=(dt == 0), stop=(dt == NDT - 1)),
                    waits=w, incs=(mmo.inc(),) if dt == NDT - 1 else ())
            lb.op("scalar", lambda e, o=o: e.activation(outT[:, o, :], ps_c[:], AF.Identity,
                                                        bias=boutc[:, o:o + 1]),
                  waits=(mmo.at(),), incs=(mme.inc(),))
            mme_after_evac.append(mme.n)
        for tj in range(TPC):
            ti = c * TPC + tj
            bi = tj % 2
            lb.op("sync", lambda e, ti=ti, bi=bi: e.dma_start(out=xin[bi][:], in_=x2[ti * 128:(ti + 1) * 128, :]),
                  waits=(st.at(),) if ti >= 2 else (), incs=(ldS[bi].inc(16),))
            for half in range(2):
                pbuf = ps_b[half]
                for q in range(4):
                    ct = half * 4 + q
                    w = (mme.at(), st.at(), ev.at()) if q == 0 else ()
                    lb.op("tensor", lambda e, tj=tj, ct=ct, q=q, pbuf=pbuf: e.transpose(
                        pbuf[:, q * 128:(q + 1) * 128], outT[:, ct, tj * 128:(tj + 1) * 128], idf[:]),
                        waits=w, incs=(pe.inc(),) if q == 3 else ())
                lb.op("scalar", lambda e, half=half, pbuf=pbuf: e.copy(
                    out_tok[:, half * 512:(half + 1) * 512], pbuf[:]),
                    waits=(pe.at(),), incs=(ev.inc(),))
            lb.op("vector", lambda e, bi=bi: e.tensor_add(out_tok[:], xin[bi][:], out_tok[:]),
                  waits=(ev.at(), ldS[bi].at()), incs=(dv.inc(),))
            lb.op("sync", lambda e, ti=ti: e.dma_start(out=x3[ti * 128:(ti + 1) * 128, :], in_=out_tok[:]),
                  waits=(dv.at(),), incs=(st.inc(16),))
    lb.op("sync", lambda e: e.nop(), waits=(st.at(),))
    lb.pop()


def decl_l1_attn(lb, x_ap=None, out_ap=None):
    ext = {}
    ext["x2"] = x_ap if x_ap is not None else lb.dram("x2", [T, C], F32, kind="ExternalInput").ap()
    ext["win"] = lb.dram("win", [C, 2 * DI], F32, kind="ExternalInput").ap()
    ext["bin"] = lb.dram("bin", [2 * DI], F32, kind="ExternalInput").ap()
    ext["wx"] = lb.dram("wx", [DI, 96], F32, kind="ExternalInput").ap()
    ext["bx"] = lb.dram("bx", [96], F32, kind="ExternalInput").ap()
    ext["wdt"] = lb.dram("wdt", [64, DI], F32, kind="ExternalInput").ap()
    ext["bdt"] = lb.dram("bdt", [DI], F32, kind="ExternalInput").ap()
    ext["alog"] = lb.dram("alog", [DI, NST], F32, kind="ExternalInput").ap()
    ext["dp_v"] = lb.dram("dp_v", [DI], F32, kind="ExternalInput").ap()
    ext["wout"] = lb.dram("wout", [DI, C], F32, kind="ExternalInput").ap()
    ext["bout"] = lb.dram("bout", [C], F32, kind="ExternalInput").ap()
    ext["sel16"] = lb.dram("sel16", [16, 16 * 128], BF16, kind="ExternalInput").ap()
    ext["x3"] = out_ap if out_ap is not None else lb.dram("x3", [T, C], F32, kind="ExternalOutput").ap()
    return ext


def build_l1_attn():
    lb = LB()
    lb.ident_ap = lb.dram("i128f", [128, 128], F32, kind="ExternalInput").ap()
    lb.init_consts([1e-5, -0.5, -1.0, 1.0, 1.0 / 1024.0, LN8])
    ext = decl_l1_attn(lb)
    cm = make_common(lb)
    emit_l1_attn(lb, cm, ext)
    return lb.build()


def host_prep_l1attn(inp, x2_core):
    sel = np.zeros((16, 16 * 128), dtype=ml_dtypes.bfloat16)
    for n in range(16):
        sel[n, n * 128:(n + 1) * 128] = 1.0
    return {
        "x2": x2_core,
        "win": inp["L1_attn_Win"], "bin": inp["L1_attn_bin"],
        "wx": inp["L1_attn_Wx"], "bx": inp["L1_attn_bx"],
        "wdt": inp["L1_attn_Wdt"], "bdt": inp["L1_attn_bdt"],
        "alog": inp["L1_attn_A_log"], "dp_v": inp["L1_attn_Dp"],
        "wout": inp["L1_attn_Wout"], "bout": inp["L1_attn_bout"],
        "sel16": sel,
        "i128f": np.eye(128, dtype=np.float32),
    }


NE = 4
NI = 2752
NIT = 22
assert NI == 128 * 21 + 64

def emit_l1_ffn(lb, cm, ext):
    lb.pfx = "D_"
    x3 = ext["x3"]; gwT = ext["gwT"]; egT = ext["egT"]; euT = ext["euT"]
    edT = ext["edT"]; sgT = ext["sgT"]; suT = ext["suT"]; sdT = ext["sdT"]
    sel4 = ext["sel4"]; x4 = ext["x4"]; z_scr = ext["z_scr"]; ed_scr = ext["ed_scr"]
    tmp = cm["tmp"]; idf = cm["idf"]
    xin = cm["xin"]; dmp = cm["dmp"]; h_t = cm["ha"]; out_tok = cm["otok"]
    lb.push()
    sem_c = cm["sem_c"]
    gw_s = lb.sb("gw_s", [128, NCT, NE], BF16)
    lb.op("gpsimd", lambda e: e.dma_start(out=gw_s[:], in_=gwT.rearrange("(a p) o -> p a o", p=128)),
          incs=(sem_c.inc(16),))
    lb.op("gpsimd", None, waits=(sem_c.at(),))
    sel_s = lb.sb("sel_s", [4, 4 * 128], BF16)
    lb.op("sync", lambda e: e.dma_start(out=sel_s[0:4, :], in_=sel4), incs=(sem_c.inc(16),))
    lb.op("sync", None, waits=(sem_c.at(),))

    spc = lb.sb("spc", [128, 256])

    ld = lb.sem("ld"); ac = lb.sem("ac"); dv = lb.sem("dv")
    pe = lb.sem("pe"); ev = lb.sem("ev"); st = lb.sem("st")
    mmo = lb.sem("mmo"); mme = lb.sem("mme"); ldw = lb.sem("ldw"); zst = lb.sem("zst")
    ldS = [lb.sem("ldS0"), lb.sem("ldS1")]
    ldw2 = [lb.sem("ldw20"), lb.sem("ldw21")]
    ldwE = [lb.sem("ldwE0"), lb.sem("ldwE1")]
    ldzE = [lb.sem(f"ldzE{i}") for i in range(5)]
    edS = lb.sem("edS")

    lb.op("vector", lambda e: e.memset(spc[:], 0.0), incs=(dv.inc(),))

    lb.push()
    h3T = lb.sb("h3T", [128, NCT, T], BF16)
    sc_all = lb.sb("sc_all", [128, NTILE * NE])
    ps_b = [lb.ps(f"ps_b{i}", [128, 512]) for i in range(2)]
    ps_gp = [lb.ps(f"ps_gp{i}", [128, 512]) for i in range(2)]
    ps_up = [lb.ps(f"ps_up{i}", [128, 512]) for i in range(2)]
    ps_g = ps_gp[0]
    ps_u = ps_up[0]

    sc3 = lambda: sc_all[:].rearrange("p (i e) -> p i e", e=NE)
    for i in range(NTILE):
        bi = i % 2
        wfree = (dv.at(), ac.at()) if i >= 2 else ()
        lb.op("sync", lambda e, i=i, bi=bi: e.dma_start(out=xin[bi][:], in_=x3[i * 128:(i + 1) * 128, :]),
              waits=wfree, incs=(ldS[bi].inc(16),))
        emit_ln(lb, tmp, xin[bi], h_t, dmp, ac, dv,
                wait_in=(ldS[bi].at(),) + ((pe.at(),) if i > 0 else ()))
        emit_transpose8(lb, h_t, lambda half, i=i: h3T[:, half * 4:(half + 1) * 4, i * 128:(i + 1) * 128],
                        ps_b, idf, pe, ev, (dv.at(), sem_c.at()))
        for ct in range(NCT):
            w = (ev.at(), mme.at()) if ct == 0 else ()
            lb.op("tensor", lambda e, i=i, ct=ct: e.matmul(
                ps_g[:, 0:NE], h3T[:, ct, i * 128:(i + 1) * 128], gw_s[:, ct, :],
                start=(ct == 0), stop=(ct == NCT - 1)),
                waits=w, incs=(mmo.inc(),) if ct == NCT - 1 else ())
        lb.op("scalar", lambda e, i=i: e.copy(sc3()[:, i, :], ps_g[:, 0:NE]),
              waits=(mmo.at(),), incs=(mme.inc(),))

    mx_t = lb.sb("mx_t", [128, NTILE]); ss_t = lb.sb("ss_t", [128, NTILE])
    m1_t = lb.sb("m1_t", [128, NTILE]); m2_t = lb.sb("m2_t", [128, NTILE])
    n_t = lb.sb("n_t", [128, NTILE]); t1g = lb.sb("t1g", [128, NTILE])
    e_t = lb.sb("e_t", [128, NTILE * NE]); p_t = lb.sb("p_t", [128, NTILE * NE])
    g1_t = lb.sb("g1_t", [128, NTILE * NE]); sel1 = lb.sb("sel1", [128, NTILE * NE])
    sel2 = lb.sb("sel2", [128, NTILE * NE]); p2_t = lb.sb("p2_t", [128, NTILE * NE])
    comb_t = lb.sb("comb_t", [128, NTILE * NE])
    combT = lb.sb("combT", [4, T], BF16)
    d_t = lb.sb("d_t", [128, NTILE])
    v4 = lambda ap: ap[:].rearrange("p (i e) -> p i e", e=NE)
    bc4 = lambda ap: ap[:].unsqueeze(2).broadcast_to([128, NTILE, NE])

    def spacer():
        lb.op("vector", lambda e: e.tensor_scalar_mul(spc[:], spc[:], 1.0))

    def first_occurrence(src, dst):
        lb.op("vector", lambda e: e.tensor_scalar(n_t[:], v4(src)[:, :, 0], -1.0, 1.0, ALU.mult, ALU.add))
        lb.op("vector", lambda e: e.tensor_scalar(t1g[:], v4(src)[:, :, 1], -1.0, 1.0, ALU.mult, ALU.add))
        lb.op("vector", lambda e: e.tensor_copy(v4(dst)[:, :, 0], v4(src)[:, :, 0]))
        lb.op("vector", lambda e: e.tensor_mul(v4(dst)[:, :, 1], v4(src)[:, :, 1], n_t[:]))
        lb.op("vector", lambda e: e.tensor_mul(n_t[:], n_t[:], t1g[:]))
        lb.op("vector", lambda e: e.tensor_scalar(t1g[:], v4(src)[:, :, 2], -1.0, 1.0, ALU.mult, ALU.add))
        lb.op("vector", lambda e: e.tensor_mul(v4(dst)[:, :, 2], v4(src)[:, :, 2], n_t[:]))
        lb.op("vector", lambda e: e.tensor_mul(n_t[:], n_t[:], t1g[:]))
        spacer()
        lb.op("vector", lambda e: e.tensor_mul(v4(dst)[:, :, 3], v4(src)[:, :, 3], n_t[:]))

    lb.op("vector", lambda e: e.reduce_max(mx_t[:], sc3(), axis=AX.X), waits=(mme.at(),))
    spacer()
    lb.op("vector", lambda e: e.tensor_sub(v4(e_t), sc3(), bc4(mx_t)), incs=(dv.inc(),))
    lb.op("scalar", lambda e: e.activation(e_t[:], e_t[:], AF.Exp), waits=(dv.at(),), incs=(ac.inc(),))
    lb.op("vector", lambda e: e.reduce_sum(ss_t[:], v4(e_t), axis=AX.X), waits=(ac.at(),))
    spacer()
    lb.op("vector", lambda e: e.reciprocal(ss_t[:], ss_t[:]))
    spacer()
    lb.op("vector", lambda e: e.tensor_mul(v4(p_t), v4(e_t), bc4(ss_t)))
    lb.op("vector", lambda e: e.reduce_max(m1_t[:], v4(p_t), axis=AX.X))
    spacer()
    lb.op("vector", lambda e: e.tensor_tensor(v4(g1_t), v4(p_t), bc4(m1_t), ALU.is_ge))
    first_occurrence(g1_t, sel1)
    lb.op("vector", lambda e: e.scalar_tensor_tensor(p2_t[:], sel1[:], -1e30, p_t[:], ALU.mult, ALU.add))
    lb.op("vector", lambda e: e.reduce_max(m2_t[:], v4(p2_t), axis=AX.X))
    spacer()
    lb.op("vector", lambda e: e.tensor_tensor(v4(g1_t), v4(p2_t), bc4(m2_t), ALU.is_ge))
    first_occurrence(g1_t, sel2)
    lb.op("vector", lambda e: e.tensor_add(d_t[:], m1_t[:], m2_t[:]))
    spacer()
    lb.op("vector", lambda e: e.tensor_scalar(d_t[:], d_t[:], 1.0, 1e-20, ALU.mult, ALU.add))
    spacer()
    lb.op("vector", lambda e: e.reciprocal(d_t[:], d_t[:]))
    spacer()
    lb.op("vector", lambda e: e.tensor_mul(m1_t[:], m1_t[:], d_t[:]))
    lb.op("vector", lambda e: e.tensor_mul(m2_t[:], m2_t[:], d_t[:]))
    spacer()
    lb.op("vector", lambda e: e.tensor_mul(v4(comb_t), v4(sel1), bc4(m1_t)))
    lb.op("vector", lambda e: e.tensor_mul(v4(e_t), v4(sel2), bc4(m2_t)))
    lb.op("vector", lambda e: e.tensor_add(comb_t[:], comb_t[:], e_t[:]), incs=(dv.inc(),))
    for i in range(NTILE):
        lb.op("tensor", lambda e, i=i: e.transpose(
            ps_g[0:NE, 0:128], v4(comb_t)[:, i, :], idf[:]),
            waits=(dv.at(), mme.at()), incs=(mmo.inc(),))
        lb.op("scalar", lambda e, i=i: e.copy(combT[0:NE, i * 128:(i + 1) * 128], ps_g[0:NE, 0:128]),
              waits=(mmo.at(),), incs=(mme.inc(),))
    cbc_s = lb.sb("cbc_s", [128, NE, NCHUNK, CHUNK], BF16)
    for e_ in range(NE):
        for c in range(NCHUNK):
            lb.op("tensor", lambda e, e_=e_, c=c: e.matmul(
                ps_g[:], sel_s[0:4, e_ * 128:(e_ + 1) * 128], combT[0:4, c * CHUNK:(c + 1) * CHUNK],
                start=True, stop=True),
                waits=(mme.at(),), incs=(mmo.inc(),))
            lb.op("scalar", lambda e, e_=e_, c=c: e.copy(cbc_s[:, e_, c, :], ps_g[:]),
                  waits=(mmo.at(),), incs=(mme.inc(),))
    lb.barrier()

    wg_s = [lb.sb(f"wg_s{i}", [128, NCT, 128], BF16) for i in range(2)]
    wu_s = [lb.sb(f"wu_s{i}", [128, NCT, 128], BF16) for i in range(2)]
    sg_b = [lb.sb(f"sg_b{i}", [128, CHUNK], BF16) for i in range(2)]
    u2_b = [lb.sb(f"u2_b{i}", [128, CHUNK], BF16) for i in range(2)]
    t2_t = lb.sb("t2_t", [128, CHUNK], BF16)
    z_bf = [lb.sb(f"z_bf{i}", [128, CHUNK], BF16) for i in range(2)]
    mme_after = []
    dv_after = []
    zsem = [lb.sem(f"z{i}") for i in range(2)]
    it = 0
    for e_ in range(5):
        for i in range(NIT):
            bi = it % 2
            rows = 128 if i < 21 else 64
            w_buf = ((mme.h, mme_after[it - 2]),) if it >= 2 else ()
            for (wsl, src) in ((wg_s, egT if e_ < 4 else sgT), (wu_s, euT if e_ < 4 else suT)):
                lb.op("gpsimd", lambda e, e_=e_, i=i, bi=bi, wsl=wsl, src=src, rows=rows: e.dma_start(
                    out=wsl[bi][:, :, 0:rows],
                    in_=(src[e_] if e_ < 4 else src).rearrange("(a p) o -> p a o", p=128)[:, :, i * 128:i * 128 + rows]),
                    waits=w_buf, incs=(ldw2[bi].inc(16),))
                w_buf = ()
            mme_c = []
            dv_c = []
            for c in range(NCHUNK):
                par = c % 2
                psg = ps_gp[par]; psu = ps_up[par]
                if c >= 2:
                    w = (ldw2[bi].at(), (mme.h, mme_c[c - 2]))
                elif it >= 1:
                    w = (ldw2[bi].at(), (mme.h, mme_after[it - 1]))
                else:
                    w = (ldw2[bi].at(),)
                for ct in range(NCT):
                    lb.op("tensor", lambda e, bi=bi, ct=ct, c=c, rows=rows, psg=psg: e.matmul(
                        psg[0:rows, :], wg_s[bi][:, ct, 0:rows], h3T[:, ct, c * CHUNK:(c + 1) * CHUNK],
                        start=(ct == 0), stop=(ct == NCT - 1)), waits=w if ct == 0 else ())
                for ct in range(NCT):
                    lb.op("tensor", lambda e, bi=bi, ct=ct, c=c, rows=rows, psu=psu: e.matmul(
                        psu[0:rows, :], wu_s[bi][:, ct, 0:rows], h3T[:, ct, c * CHUNK:(c + 1) * CHUNK],
                        start=(ct == 0), stop=(ct == NCT - 1)),
                        incs=(mmo.inc(),) if ct == NCT - 1 else ())
                wact = (mmo.at(),)
                if c >= 2:
                    wact = wact + ((dv.h, dv_c[c - 2]),)
                elif it >= 1:
                    wact = wact + ((dv.h, dv_after[it - 1]),)
                lb.op("scalar", lambda e, rows=rows, psg=psg, par=par: e.activation(
                    sg_b[par][0:rows, :], psg[0:rows, :], AF.Silu), waits=wact)
                lb.op("scalar", lambda e, rows=rows, psu=psu, par=par: e.copy(
                    u2_b[par][0:rows, :], psu[0:rows, :]), incs=(mme.inc(),))
                mme_c.append(mme.n)
                zslot = c % 2
                lb.op("vector", lambda e, rows=rows, par=par: e.tensor_mul(
                    t2_t[0:rows, :], sg_b[par][0:rows, :], u2_b[par][0:rows, :]),
                    waits=(mme.at(), (zsem[zslot].h, zsem[zslot].n)))
                if e_ < 4:
                    lb.op("vector", lambda e, e_=e_, c=c, zslot=zslot, rows=rows: e.tensor_mul(
                        z_bf[zslot][0:rows, :], t2_t[0:rows, :], cbc_s[0:rows, e_, c, :]),
                        incs=(dv.inc(),))
                else:
                    lb.op("vector", lambda e, zslot=zslot, rows=rows: e.tensor_scalar_mul(
                        z_bf[zslot][0:rows, :], t2_t[0:rows, :], 1.0), incs=(dv.inc(),))
                dv_c.append(dv.n)
                lb.op("sync", lambda e, e_=e_, i=i, c=c, zslot=zslot, rows=rows: e.dma_start(
                    out=z_scr[e_, i, 0:rows, c * CHUNK:(c + 1) * CHUNK], in_=z_bf[zslot][0:rows, :]),
                    waits=(dv.at(),), incs=(zsem[zslot].inc(16),))
            mme_after.append(mme.n)
            dv_after.append(dv.n)
            it += 1
    lb.op("sync", lambda e: e.nop(), waits=(zsem[0].at(), zsem[1].at()))
    lb.barrier()
    lb.pop()

    lb.push()
    z_sb = lb.sb("z_sb", [128, 5 * NIT, CHUNK], BF16)
    ed_sb = [lb.sb(f"ed_sb{i}", [128, 5 * NIT, 128], BF16) for i in range(2)]
    outT = lb.sb("outT", [128, NCT, CHUNK])
    ps_y = lb.ps("ps_y", [128, 512])
    ps_t = [lb.ps(f"ps_t{i}", [128, 512]) for i in range(2)]
    ldz = lb.sem("ldz")
    mme_after_o = []
    it = 0
    for c in range(NCHUNK):
        if c > 0:
            lb.barrier()
        ldz_hist = []
        for e_ in range(5):
            src = edT if e_ < 4 else sdT
            lb.op("sync", lambda e, e_=e_, c=c, src=src: e.dma_start(
                out=z_sb[:, e_ * NIT:(e_ + 1) * NIT, :],
                in_=z_scr[e_].rearrange("i p t -> p i t")[:, :, c * CHUNK:(c + 1) * CHUNK]),
                incs=(ldzE[e_].inc(16),))
            ldz_hist.append((ldzE[e_].h, ldzE[e_].n))
        for o in range(NCT):
            bi = it % 2
            w_buf = ((mme.h, mme_after_o[it - 2]),) if it >= 2 else ()
            if c == 0:
                if o >= 2:
                    w_buf = w_buf + ((edS.h, 16 * (o - 1)),)
                for e_ in range(5):
                    src = edT[e_] if e_ < 4 else sdT
                    lb.op("gpsimd", lambda e, o=o, bi=bi, src=src, e_=e_: e.dma_start(
                        out=ed_sb[bi][:, e_ * NIT:e_ * NIT + 21, :],
                        in_=src[0:2688, :].rearrange("(a p) o -> p a o", p=128)[:, :, o * 128:(o + 1) * 128]),
                        waits=w_buf, incs=(ldwE[bi].inc(16),))
                    w_buf = ()
                    lb.op("gpsimd", lambda e, o=o, bi=bi, src=src, e_=e_: e.dma_start(
                        out=ed_sb[bi][0:64, e_ * NIT + 21, :],
                        in_=src[2688:2752, o * 128:(o + 1) * 128]),
                        incs=(ldwE[bi].inc(16),))
                lb.op("sync", lambda e, o=o, bi=bi: e.dma_start(
                    out=ed_scr[o], in_=ed_sb[bi][:]),
                    waits=(ldwE[bi].at(),), incs=(edS.inc(16),))
            else:
                if c == 1 and o == 0:
                    w_buf = w_buf + (edS.at(),)
                lb.op("gpsimd", lambda e, o=o, bi=bi: e.dma_start(
                    out=ed_sb[bi][:], in_=ed_scr[o]),
                    waits=w_buf, incs=(ldwE[bi].inc(16),))
            for k in range(5 * NIT):
                e_i, i_i = divmod(k, NIT)
                rows = 128 if i_i < 21 else 64
                w = ()
                if k == 0:
                    w = (ldz_hist[0], ldwE[bi].at(), mme.at())
                elif i_i == 0:
                    w = (ldz_hist[e_i],)
                lb.op("tensor", lambda e, k=k, bi=bi, rows=rows: e.matmul(
                    ps_y[:], ed_sb[bi][0:rows, k, :], z_sb[0:rows, k, :],
                    start=(k == 0), stop=(k == 5 * NIT - 1)),
                    waits=w, incs=(mmo.inc(),) if k == 5 * NIT - 1 else ())
            lb.op("scalar", lambda e, o=o: e.copy(outT[:, o, :], ps_y[:]),
                  waits=(mmo.at(),), incs=(mme.inc(),))
            mme_after_o.append(mme.n)
            it += 1
        for tj in range(TPC):
            ti = c * TPC + tj
            bi = tj % 2
            lb.op("sync", lambda e, ti=ti, bi=bi: e.dma_start(out=xin[bi][:], in_=x3[ti * 128:(ti + 1) * 128, :]),
                  waits=(st.at(),) if ti >= 2 else (), incs=(ldS[bi].inc(16),))
            for half in range(2):
                pbuf = ps_t[half]
                for q in range(4):
                    ct = half * 4 + q
                    w = (mme.at(), st.at(), ev.at()) if q == 0 else ()
                    lb.op("tensor", lambda e, tj=tj, ct=ct, q=q, pbuf=pbuf: e.transpose(
                        pbuf[:, q * 128:(q + 1) * 128], outT[:, ct, tj * 128:(tj + 1) * 128], idf[:]),
                        waits=w, incs=(pe.inc(),) if q == 3 else ())
                lb.op("scalar", lambda e, half=half, pbuf=pbuf: e.copy(
                    out_tok[:, half * 512:(half + 1) * 512], pbuf[:]),
                    waits=(pe.at(),), incs=(ev.inc(),))
            lb.op("vector", lambda e, bi=bi: e.tensor_add(out_tok[:], xin[bi][:], out_tok[:]),
                  waits=(ev.at(), ldS[bi].at()), incs=(dv.inc(),))
            lb.op("sync", lambda e, ti=ti: e.dma_start(out=x4[ti * 128:(ti + 1) * 128, :], in_=out_tok[:]),
                  waits=(dv.at(),), incs=(st.inc(16),))
    lb.op("sync", lambda e: e.nop(), waits=(st.at(),))
    lb.pop()
    lb.pop()


def decl_l1_ffn(lb, x_ap=None, out_ap=None):
    ext = {}
    ext["x3"] = x_ap if x_ap is not None else lb.dram("x3", [T, C], F32, kind="ExternalInput").ap()
    ext["gwT"] = lb.dram("gwT", [C, NE], F32, kind="ExternalInput").ap()
    ext["egT"] = lb.dram("egT", [NE, C, NI], F32, kind="ExternalInput").ap()
    ext["euT"] = lb.dram("euT", [NE, C, NI], F32, kind="ExternalInput").ap()
    ext["edT"] = lb.dram("edT", [NE, NI, C], F32, kind="ExternalInput").ap()
    ext["sgT"] = lb.dram("sgT", [C, NI], F32, kind="ExternalInput").ap()
    ext["suT"] = lb.dram("suT", [C, NI], F32, kind="ExternalInput").ap()
    ext["sdT"] = lb.dram("sdT", [NI, C], F32, kind="ExternalInput").ap()
    ext["sel4"] = lb.dram("sel4", [4, 4 * 128], BF16, kind="ExternalInput").ap()
    ext["x4"] = out_ap if out_ap is not None else lb.dram("x4", [T, C], F32, kind="ExternalOutput").ap()
    ext["z_scr"] = lb.dram("z_scr", [5, NIT, 128, T], BF16, kind="Internal").ap()
    ext["ed_scr"] = lb.dram("ed_scr", [NCT, 128, 5 * NIT * 128], BF16, kind="Internal").ap()
    return ext


def build_l1_ffn():
    lb = LB()
    lb.ident_ap = lb.dram("i128f", [128, 128], F32, kind="ExternalInput").ap()
    lb.init_consts([1e-5, -0.5, -1.0, 1.0, 1.0 / 1024.0, LN8])
    ext = decl_l1_ffn(lb)
    cm = make_common(lb)
    emit_l1_ffn(lb, cm, ext)
    return lb.build()


def host_prep_l1ffn(inp, x3_core):
    ct = np.ascontiguousarray
    sel = np.zeros((4, 4 * 128), dtype=ml_dtypes.bfloat16)
    for n in range(4):
        sel[n, n * 128:(n + 1) * 128] = 1.0
    return {
        "x3": x3_core,
        "gwT": ct(inp["L1_ffn_gate_w"].T),
        "egT": ct(inp["L1_ffn_eg"].transpose(0, 2, 1)),
        "euT": ct(inp["L1_ffn_eu"].transpose(0, 2, 1)),
        "edT": ct(inp["L1_ffn_ed"].transpose(0, 2, 1)),
        "sgT": ct(inp["L1_ffn_sg"].T), "suT": ct(inp["L1_ffn_su"].T),
        "sdT": ct(inp["L1_ffn_sd"].T),
        "sel4": sel,
        "i128f": np.eye(128, dtype=np.float32),
    }



def build_fused(debug=False):
    lb = LB()
    kind = "ExternalOutput" if debug else "Internal"
    lb.ident_ap = lb.dram("i128f", [128, 128], F32, kind="ExternalInput").ap()
    lb.init_consts([1e-5, -0.5, -1.0, 1.0, 1.0 / 1024.0, LN8])
    x1 = lb.dram("x1", [T, C], F32, kind=kind).ap()
    x2 = lb.dram("x2", [T, C], F32, kind=kind).ap()
    x3 = lb.dram("x3", [T, C], F32, kind=kind).ap()
    extA = decl_l0_attn(lb, out_ap=x1)
    extB = decl_l0_ffn(lb, x_ap=x1, out_ap=x2)
    extC = decl_l1_attn(lb, x_ap=x2, out_ap=x3)
    extD = decl_l1_ffn(lb, x_ap=x3)
    cm = make_common(lb)
    emit_l0_attn(lb, cm, extA)
    lb.barrier()
    emit_l0_ffn(lb, cm, extB)
    lb.barrier()
    emit_l1_attn(lb, cm, extC)
    lb.barrier()
    emit_l1_ffn(lb, cm, extD)
    return lb.build()


def host_prep_fused(inp, x_core):
    m = {}
    m.update(host_prep_l0attn(inp, x_core))
    m.update(host_prep_l0ffn(inp, None)); m.pop("x1")
    m.update(host_prep_l1attn(inp, None)); m.pop("x2")
    m.update(host_prep_l1ffn(inp, None)); m.pop("x3")
    m["x0"] = x_core
    return m


def kernel(**inputs):
    x = np.ascontiguousarray(np.asarray(inputs["L0_attn_x"], dtype=np.float32))
    assert x.shape == (8, T, C)
    nc = build_fused()
    prep = host_prep_fused(inputs, x[0])
    maps = [{**prep, "x0": np.ascontiguousarray(x[c])} for c in range(8)]
    res = run_bass_kernel_spmd(nc, maps, core_ids=list(range(8)))
    return np.stack([np.asarray(r["x4"]) for r in res.results], axis=0)
